# version 18
# baseline (speedup 1.0000x reference)
import os
import numpy as np
import ml_dtypes
import concourse.bass as bass
import concourse.mybir as mybir
from concourse.bass_utils import run_bass_kernel_spmd
from contextlib import ExitStack

F32 = mybir.dt.float32
BF16 = mybir.dt.bfloat16
AX = mybir.AxisListType.X
AF = mybir.ActivationFunctionType
ALU = mybir.AluOpType

NCORES = 8
TOK = 16384
TPC = TOK // NCORES
CH = 1024
NCH = TPC // CH
TAU = CH // 128
KT = 32
NB = 8
DM = 4096
EPS = 1e-5


def _build_program():
    nc = bass.Bass("TRN2", target_bir_lowering=False,
                   detect_race_conditions=bool(os.environ.get("KRACE")))

    x_d = nc.dram_tensor("x", [TPC, DM], F32, kind="ExternalInput").ap()
    wT = {nm: nc.dram_tensor(nm + "T", [DM, DM], F32, kind="ExternalInput").ap()
          for nm in ("wq", "wk", "wv", "wo", "wb")}
    wrT_d = nc.dram_tensor("wrT", [DM, 8], F32, kind="ExternalInput").ap()
    aT_d = nc.dram_tensor("aT", [DM, 128], F32, kind="ExternalInput").ap()
    bfT_d = nc.dram_tensor("bfT", [128, DM], F32, kind="ExternalInput").ap()
    bb_d = nc.dram_tensor("bb", [1, DM], F32, kind="ExternalInput").ap()
    sm_d = nc.dram_tensor("smalls", [1, 384], F32, kind="ExternalInput").ap()
    out_d = nc.dram_tensor("out", [TPC, DM], F32, kind="ExternalOutput").ap()

    ident_d = nc.inline_tensor(np.eye(128, dtype=ml_dtypes.bfloat16), "identc").ap()
    REPc = np.zeros((8, 128), dtype=ml_dtypes.bfloat16)
    for e in range(8):
        REPc[e, 16 * e:16 * e + 16] = 2.0
    rep_d = nc.inline_tensor(REPc, "repc").ap()

    wS = {nm: nc.dram_tensor("s_" + nm, [NB, 8, 128, 2048], BF16, kind="Internal").ap()
          for nm in ("wq", "wk", "wv", "wo", "wb")}
    qn_d = nc.dram_tensor("qn", [CH, DM], BF16, kind="Internal").ap()
    kn_d = nc.dram_tensor("kn", [CH, DM], BF16, kind="Internal").ap()
    vn_d = nc.dram_tensor("vn", [CH, DM], BF16, kind="Internal").ap()
    on_d = nc.dram_tensor("on", [CH, DM], BF16, kind="Internal").ap()
    ap_d = nc.dram_tensor("apn", [CH, DM], F32, kind="Internal").ap()
    yn_d = nc.dram_tensor("yn", [CH, DM], F32, kind="Internal").ap()
    xp_d = nc.dram_tensor("xp", [TPC, DM], F32, kind="Internal").ap()

    ctx = ExitStack()
    sb = lambda n, sh, dt: ctx.enter_context(nc.sbuf_tensor(n, sh, dt))

    LNT = sb("LNT", [128, KT * 1024], BF16)
    WG = [sb(f"WG{i}", [128, 2048], BF16) for i in range(4)]
    QST = sb("QST", [128, 2048], BF16)
    APST = sb("APST", [128, 2048], F32)
    QSG = [sb(f"QSG{i}", [128, 4096], BF16) for i in range(2)]
    KB = [sb(f"KB{i}", [128, 2048], BF16) for i in range(2)]
    VB = [sb(f"VB{i}", [128, 2048], BF16) for i in range(2)]
    OB = [sb(f"OB{i}", [128, 2048], BF16) for i in range(2)]
    QBD = [sb(f"QBD{i}", [128, 2048], BF16) for i in range(2)]
    KTS = [sb("KTS0", [128, 1024], BF16)]
    PSTG = [sb(f"PSTG{i}", [128, 2048], BF16) for i in range(2)]
    XT = sb("XT", [128, DM], F32)
    T2 = sb("T2", [128, DM], F32)
    BFN = sb("BFN", [128, DM], BF16)
    idt = sb("idt", [128, 128], BF16)
    rep = sb("rep", [8, 128], BF16)
    onesb = sb("onesb", [1, 128], BF16)
    onesf = sb("onesf", [1, 128], F32)
    smw = sb("smw", [1, 384], F32)
    mrow = sb("mrow", [1, 128], F32)
    maskK = sb("maskK", [128, 64], F32)
    maskV = sb("maskV", [128, 64], F32)
    wrT_s = sb("wrT_s", [128, KT * 8], BF16)
    bbn = [sb(f"bbn{i}", [1, 512], BF16) for i in range(2)]
    bfTn = [sb(f"bfTn{i}", [128, 512], BF16) for i in range(2)]
    BN = sb("BN", [128, 48], F32)
    MV = sb("MV", [128, 2], F32)
    RSTD = sb("RSTD", [128, 1], F32)
    NMR = sb("NMR", [128, 1], F32)
    NM = sb("NM", [128, 1], F32)
    NM2 = sb("NM2", [128, 1], F32)
    EPSC = sb("EPSC", [128, 1], F32)
    NHALF = sb("NHALF", [128, 1], F32)
    ZW = sb("ZW", [128, 16], F32)
    RZW = sb("RZW", [128, 16], F32)
    LGT = sb("LGT", [128, 8], F32)
    SRT = sb("SRT", [128, 8], F32)
    EQ1 = sb("EQ1", [128, 8], F32)
    EQ2 = sb("EQ2", [128, 8], F32)
    WFV = sb("WFV", [128, 8], F32)
    WBF = sb("WBF", [128, 8], BF16)
    CC = sb("CC", [128, 8], F32)
    WTS = sb("WTS", [8, 1024], BF16)
    WEX = sb("WEX", [128, 1024], BF16)
    HWT = sb("HWT", [128, 1024], BF16)
    PALL = ctx.enter_context(nc.psum_tensor("PALL", [128, 4096], F32))

    DMASEMS = (["wg0", "wg1", "wg2", "wg3", "wgu0", "wgu1", "wgu2", "wgu3",
                "sw0", "sw1", "sw2", "sw3", "bbl0", "bbl1", "qs0", "qs1", "qs2", "qs3",
                "rb0", "rb1", "ob0", "ob1", "onr", "xl", "t2l", "xps", "outs",
                "ap0", "ap1", "ap2", "ap3", "setq", "setu", "bfl0", "bfl1"])
    sems = {k: ctx.enter_context(nc.semaphore("sem" + k)) for k in list("TVSG") + DMASEMS}
    import sys
    print("SBUF remaining/partition after allocs:", nc.sbuf_bytes_remaining, file=sys.stderr)
    blk = ctx.enter_context(nc.Block())

    sched = {k: [] for k in ("tensor", "vector", "scalar", "gpsimd", "sync")}
    cnt = {k: 0 for k in (list("TVSG") + DMASEMS)}
    seen = {e: {k: 0 for k in cnt} for e in sched}
    ESEM = {"tensor": "T", "vector": "V", "scalar": "S", "gpsimd": "G"}

    def emit(eng, fn, waits=(), inc=None, amt=1):
        w = []
        for s_, v_ in waits:
            assert v_ <= cnt[s_], f"{eng}: wait on future {s_}={v_}>{cnt[s_]}"
            if v_ > seen[eng][s_]:
                seen[eng][s_] = v_
                w.append((s_, v_))
        if inc is not None:
            cnt[inc] += amt
        sched[eng].append((tuple(w), fn, inc, amt))
        return cnt[inc] if inc is not None else None

    guards = {}

    def g_w(*keys):
        out = []
        for k in keys:
            out += guards.get(k, [])
        return out

    def g_set(key, *pairs):
        guards[key] = list(pairs)

    def g_add(key, *pairs):
        guards.setdefault(key, []).extend(pairs)

    def dma(eng, out, in_, waits=(), sem=None):
        return emit(eng, lambda e, o=out, i=in_: e.dma_start(out=o, in_=i),
                    waits, inc=sem, amt=16)

    pb = lambda b, off, w: PALL[:, b * 512 + off: b * 512 + off + w]

    dq0_ = dma("sync", idt[:], ident_d, sem="setq")
    dma("sync", rep[:], rep_d, sem="setq")
    dq_sm_ = dma("sync", smw[:], sm_d, sem="setq")
    dma("gpsimd", wrT_s[:].rearrange("p (kt e) -> p kt e", e=8),
        wrT_d.rearrange("(kt p) e -> p kt e", p=128), sem="setu")
    emit("gpsimd", lambda e: e.memset(onesb[:], 1.0), inc="G")
    emit("gpsimd", lambda e: e.memset(onesf[:], 1.0), inc="G")
    emit("gpsimd", lambda e: e.memset(PSTG[0][:], 0.0), inc="G")
    emit("gpsimd", lambda e: e.memset(PSTG[1][:], 0.0), inc="G")
    emit("gpsimd", lambda e: e.memset(QSG[0][:], 0.0), inc="G")
    emit("gpsimd", lambda e: e.memset(QSG[1][:], 0.0), inc="G")
    emit("gpsimd", lambda e: e.memset(EPSC[:], EPS), inc="G")
    g0 = emit("gpsimd", lambda e: e.memset(NHALF[:], -0.5), inc="G")

    v1 = emit("vector", lambda e: e.tensor_mul(mrow[:, 0:64], smw[:, 128:192], smw[:, 64:128]),
              [("setq", 48)], inc="V")
    v1 = emit("vector", lambda e: e.tensor_add(mrow[:, 0:64], mrow[:, 0:64], smw[:, 0:64]),
              [("V", v1)], inc="V")
    emit("vector", lambda e: e.tensor_mul(mrow[:, 64:128], smw[:, 320:384], smw[:, 256:320]),
         inc="V")
    v1 = emit("vector", lambda e: e.tensor_add(mrow[:, 64:128], mrow[:, 64:128], smw[:, 192:256]),
              [("V", cnt["V"])], inc="V")
    s1 = emit("scalar", lambda e: e.activation(mrow[:], mrow[:], AF.Sigmoid),
              [("V", v1)], inc="S")
    v1 = emit("vector", lambda e: e.tensor_mul(mrow[:, 0:64], mrow[:, 0:64], mrow[:, 64:128]),
              [("S", s1)], inc="V")
    v1 = emit("vector", lambda e: e.tensor_scalar_mul(mrow[:, 0:64], mrow[:, 0:64], 0.125),
              [("V", v1)], inc="V")
    emit("tensor", lambda e: e.matmul(PALL[:, 0:64], onesf[:], mrow[:, 0:64],
                                      start=True, stop=True), [("V", v1), ("setq", 48), ("G", g0)])
    t1 = emit("tensor", lambda e: e.matmul(PALL[:, 512:576], onesf[:], mrow[:, 64:128],
                                           start=True, stop=True), inc="T")
    emit("vector", lambda e: e.tensor_copy(maskK[:], PALL[:, 0:64]), [("T", t1)])
    v_mask = emit("vector", lambda e: e.tensor_copy(maskV[:], PALL[:, 512:576]), inc="V")
    for i in range(8):
        g_set(f"bank{i}", ("V", v_mask))

    def lnorm(src_ap, dst_ap, vwaits=()):
        for i in range(8):
            emit("vector", lambda e, i=i: e.bn_stats(BN[:, 6 * i:6 * i + 6],
                                                     src_ap[:, 512 * i:512 * i + 512]),
                 (list(vwaits) + g_w("MV") + g_w("BN")) if i == 0 else (),
                 inc="V" if i == 7 else None)
        v_bn = cnt["V"]
        va = emit("vector", lambda e: e.bn_aggr(MV[:], BN[:]), [("V", v_bn)], inc="V")
        s_ln = emit("scalar", lambda e: e.activation(RSTD[:], MV[:, 1:2], AF.Ln, bias=EPSC[:, 0:1]),
                    [("V", va)] + g_w("RSTD"), inc="S")
        s_ = emit("scalar", lambda e: e.activation(RSTD[:], RSTD[:], AF.Exp, scale=NHALF[:, 0:1]),
                  [("S", s_ln)], inc="S")
        v_n = emit("vector", lambda e: e.tensor_scalar_mul(NMR[:], MV[:, 0:1], -1.0),
                   [("S", s_)] + g_w("NMR"), inc="V")
        av = emit("vector", lambda e: e.tensor_scalar(dst_ap, src_ap, NMR[:, 0:1],
                                                      RSTD[:, 0:1], ALU.add, ALU.mult),
                  [("V", v_n)], inc="V")
        g_set("BN", ("V", va))
        g_set("MV", ("S", s_ln), ("V", v_n))
        g_set("RSTD", ("V", av))
        g_set("NMR", ("V", av))
        return av

    def transpose_tile(src_sb, dst_col, first_waits=()):
        for w in range(4):
            for i in range(8):
                kt = w * 8 + i
                waits = g_w(f"bank{i}") + (list(first_waits) if kt == 0 else [])
                emit("tensor", lambda e, i=i, kt=kt: e.matmul(
                    pb(i, 0, 128), src_sb[:, kt * 128:kt * 128 + 128], idt[:],
                    start=True, stop=True), waits, inc="T" if i == 7 else None)
            tv = cnt["T"]
            eng = "vector" if w % 2 == 0 else "scalar"
            for i in range(8):
                kt = w * 8 + i
                if eng == "vector":
                    fn = (lambda e, i=i, kt=kt: e.tensor_copy(
                        LNT[:, dst_col(kt):dst_col(kt) + 128], pb(i, 0, 128)))
                else:
                    fn = (lambda e, i=i, kt=kt: e.copy(
                        LNT[:, dst_col(kt):dst_col(kt) + 128], pb(i, 0, 128)))
                emit(eng, fn, [("T", tv)] if i == 0 else [],
                     inc=ESEM[eng] if i == 7 else None)
            ev = cnt[ESEM[eng]]
            for i in range(8):
                g_set(f"bank{i}", (ESEM[eng], ev))
        return cnt["V"], cnt["S"]

    wg_idx = [0]

    def load_wgroup(nm, n, ktg, kind):
        b = wg_idx[0] % 4
        wg_idx[0] += 1
        waits = g_w(f"WG{b}")
        if kind == "cast_save":
            src = wT[nm].rearrange("(a p) c -> a p c", p=128)[
                4 * ktg:4 * ktg + 4, :, 512 * n:512 * n + 512].transpose([1, 0, 2])
            uv = dma("gpsimd", WG[b][:].rearrange("p (a c) -> p a c", a=4), src, waits,
                     sem=f"wgu{b}")
            rdy = (f"wgu{b}", uv)
            qv = dma("sync", wS[nm][n, ktg], WG[b][:], [rdy], sem=f"sw{b}")
            g_set(f"WG{b}", (f"sw{b}", qv))
        else:
            rdy = (f"wg{b}", dma("sync", WG[b][:], wS[nm][n, ktg], waits, sem=f"wg{b}"))
            g_set(f"WG{b}")
        return b, rdy

    def wg_done(b, sem, val):
        g_add(f"WG{b}", (sem, val))

    KPH = int(os.environ.get("KPHASE", "99"))

    class _Stop(Exception):
        pass

    def ph(k, c):
        if c * 10 + k > KPH:
            raise _Stop()

    def transpose_tile(src_sb, dst_col, first_waits=(), banks=(0, 1, 2, 3, 4, 5, 6, 7)):
        nb = len(banks)
        nwav = 32 // (nb * 4)
        for w in range(nwav):
            for i_, bk in enumerate(banks):
                for off in range(4):
                    kt = (w * nb + i_) * 4 + off
                    waits = (g_w(f"bank{bk}") if off == 0 else []) + \
                            (list(first_waits) if (w == 0 and i_ == 0 and off == 0) else [])
                    emit("tensor", lambda e, bk=bk, kt=kt, off=off: e.matmul(
                        pb(bk, off * 128, 128), src_sb[:, kt * 128:kt * 128 + 128], idt[:],
                        start=True, stop=True), waits,
                        inc="T" if off == 3 else None)
                tv = cnt["T"]
                eng = "vector" if (w * nb + i_) % 2 == 0 else "scalar"
                kb = (w * nb + i_) * 4
                dst = LNT[:].rearrange("p (kt x) -> p kt x", x=1024)[
                    :, kb:kb + 4, dst_col(0) - 0:dst_col(0) + 128] if False else None
                base = dst_col(kb) - kb * 1024
                dap = LNT[:].rearrange("p (kt x) -> p kt x", x=1024)[:, kb:kb + 4, base:base + 128]
                if eng == "vector":
                    fn = (lambda e, bk=bk, dap=dap: e.tensor_copy(dap, pb(bk, 0, 512)))
                else:
                    fn = (lambda e, bk=bk, dap=dap: e.copy(dap, pb(bk, 0, 512)))
                ev = emit(eng, fn, [("T", tv)], inc=ESEM[eng])
                g_set(f"bank{bk}", (ESEM[eng], ev))
        return cnt["V"], cnt["S"]

    for c in range(NCH):
      try:
        row0 = c * CH
        colk = lambda kt, tau: kt * CH + tau * 128
        src_kind0 = "cast_save" if c == 0 else "scratch"

        ph(1, c)
        for tau in range(TAU):
            xbuf, bname = (XT, "XT") if tau % 2 == 0 else (T2, "T2")
            lw = g_w(bname)
            if c == 0 and tau == 0:
                lw += [("setq", 48), ("G", g0)]
            qsem = "xl" if tau % 2 == 0 else "t2l"
            uq = dma("gpsimd", xbuf[:], x_d[row0 + tau * 128: row0 + tau * 128 + 128, :], lw,
                     sem=qsem)
            av = lnorm(xbuf[:], BFN[:], [(qsem, uq)] + g_w("BFN"))
            g_set(bname, ("V", av))
            fw = [("V", av)] + (g_w("LNTa", "LNTb") if tau == 0 else [])
            transpose_tile(BFN, lambda kt, tau=tau: colk(kt, tau), fw)
            g_set("BFN", ("T", cnt["T"]))
        a1_v, a1_s = cnt["V"], cnt["S"]

        ph(2, c)
        img = {"wq": qn_d, "wk": kn_d, "wv": vn_d}
        mask = {"wq": None, "wk": maskK, "wv": maskV}
        qst_i = [0]
        for nm in ("wq", "wk", "wv"):
            for n in range(NB):
                for ktg in range(8):
                    b, rdy = load_wgroup(nm, n, ktg, src_kind0)
                    for k4 in range(4):
                        kt = ktg * 4 + k4
                        for tau in range(TAU):
                            waits = []
                            if k4 == 0 and tau == 0:
                                waits.append(rdy)
                            if ktg == 0 and k4 == 0:
                                waits += g_w(f"bank{tau}")
                                if nm == "wq" and n == 0 and tau == 0:
                                    waits += [("V", a1_v), ("S", a1_s)]
                            last = (ktg == 7 and k4 == 3)
                            emit("tensor",
                                 lambda e, kt=kt, tau=tau, b=b, k4=k4,
                                        st=(ktg == 0 and k4 == 0), sp=last:
                                 e.matmul(pb(tau, 0, 512),
                                          LNT[:, colk(kt, tau):colk(kt, tau) + 128],
                                          WG[b][:, k4 * 512:k4 * 512 + 512],
                                          start=st, stop=sp),
                                 waits, inc="T" if (k4 == 3 and tau == TAU - 1) else None)
                    wg_done(b, "T", cnt["T"])
                tmm = cnt["T"]
                for tau in range(TAU):
                    sl = qst_i[0] % 4
                    qst_i[0] += 1
                    waits = ([("T", tmm)] if tau == 0 else []) + g_w(f"QSTs{sl}")
                    if mask[nm] is None:
                        ev = emit("scalar", lambda e, tau=tau, sl=sl: e.activation(
                            QST[:, sl * 512:sl * 512 + 512], pb(tau, 0, 512), AF.Copy),
                            waits, inc="S")
                        esem = "S"
                    else:
                        mk = mask[nm]
                        ev = emit("vector", lambda e, tau=tau, sl=sl, mk=mk: e.tensor_tensor(
                            QST[:, sl * 512:sl * 512 + 512].rearrange("p (h d) -> p h d", d=64),
                            pb(tau, 0, 512).rearrange("p (h d) -> p h d", d=64),
                            mk[:].rearrange("p (o d) -> p o d", o=1).broadcast_to([128, 8, 64]),
                            ALU.mult), waits + [("V", v_mask)], inc="V")
                        esem = "V"
                    g_set(f"bank{tau}", (esem, ev))
                    qv = dma("sync", img[nm][tau * 128:tau * 128 + 128, 512 * n:512 * n + 512],
                             QST[:, sl * 512:sl * 512 + 512], [(esem, ev)], sem=f"qs{sl}")
                    g_set(f"QSTs{sl}", (f"qs{sl}", qv))
        g_set("LNTa", ("T", cnt["T"]))
        g_set("LNTb", ("T", cnt["T"]))
        a2_img_q = [(f"qs{i}", cnt[f"qs{i}"]) for i in range(4)]

        def attn_batch(tau, bt, wavesz, banks):
            p = bt
            base0 = tau * 128 + bt * 32
            w_first = list(a2_img_q) + [("G", g0)]
            for a in range(2):
                dst = QSG[p][64 * a:64 * a + 64, :].rearrange("h (j x) -> h j x", x=128)[:, :, 64 * a:64 * a + 64]
                src = qn_d[base0 + 64 * a: base0 + 64 * a + 32, :].rearrange("t (h d) -> h t d", h=64)
                dma("sync", dst, src, w_first + g_w(f"QSG{p}") if a == 0 else [], sem=f"rb{p}")
            for a in range(2):
                for jp in range(2):
                    dst = KB[p][64 * jp:64 * jp + 64, :].rearrange("g (J x) -> g J x", x=128)[:, :, 64 * a:64 * a + 64]
                    src = kn_d[base0 + 64 * a + jp: base0 + 64 * a + 32: 2, :].rearrange("t (g d) -> g t d", g=64)
                    dma("sync", dst, src, g_w(f"KB{p}") if (a == 0 and jp == 0) else [], sem=f"rb{p}")
            for a in range(2):
                dst = VB[p][64 * a:64 * a + 64, :].rearrange("g (j d) -> g j d", d=64)
                src = vn_d[base0 + 64 * a: base0 + 64 * a + 32, :].rearrange("t (g d) -> g t d", g=64)
                dma("sync", dst, src, g_w(f"VB{p}") if a == 0 else [], sem=f"rb{p}")
            rbq = (f"rb{p}", cnt[f"rb{p}"])
            yield
            nwav = 32 // wavesz
            t_s0f = None
            for w in range(nwav):
                pw = w % 2
                if wavesz == 16:
                    qb_bk = banks[0:4]; kt_bk = banks[4:6]; s_bk = banks[6:8]
                    pt_bk = banks[0:4]; o_bk = banks[4:6]
                else:
                    qb_bk = banks[0:2]; kt_bk = banks[2:4]; s_bk = [banks[0]]
                    pt_bk = banks[2:4]; o_bk = [banks[1]]
                qpb = 4
                for j in range(wavesz):
                    jj = w * wavesz + j
                    waits = ([rbq] + g_w(*[f"bank{bk}" for bk in qb_bk])
                             + g_w(f"QBD{pw}")) if j == 0 else []
                    emit("tensor", lambda e, j=j, jj=jj, p=p, qb_bk=qb_bk: e.matmul(
                        pb(qb_bk[j // 4], (j % 4) * 128, 128), QSG[p][:, jj * 128:jj * 128 + 128],
                        idt[:], start=True, stop=True), waits,
                        inc="T" if j == wavesz - 1 else None)
                t_s0 = cnt["T"]
                if t_s0f is None:
                    t_s0f = t_s0
                yield
                for i_, bk in enumerate(qb_bk):
                    emit("vector", lambda e, pw=pw, bk=bk, i_=i_: e.tensor_copy(
                        QBD[pw][:, i_ * 512:i_ * 512 + 512], pb(bk, 0, 512)),
                        [("T", t_s0)] if i_ == 0 else [], inc="V" if i_ == len(qb_bk) - 1 else None)
                v_s1 = cnt["V"]
                for bk in qb_bk:
                    g_set(f"bank{bk}", ("V", v_s1))
                for j in range(wavesz):
                    jj = w * wavesz + j
                    waits = (g_w(*[f"bank{bk}" for bk in kt_bk]) + [rbq]) if j == 0 else []
                    ko_bk = kt_bk[jj % 2]
                    ko = (j // 2) * 64
                    emit("tensor", lambda e, j=j, jj=jj, p=p, ko_bk=ko_bk, ko=ko: e.matmul(
                        pb(ko_bk, ko, 64),
                        KB[p][64 * (jj % 2):64 * (jj % 2) + 64, (jj // 2) * 128:(jj // 2) * 128 + 128],
                        idt[64 * (jj % 2):64 * (jj % 2) + 64, 64 * (jj % 2):64 * (jj % 2) + 64],
                        start=True, stop=True), waits, inc="T" if j == wavesz - 1 else None)
                t_s2 = cnt["T"]
                yield
                for i_, bk in enumerate(kt_bk):
                    emit("scalar", lambda e, bk=bk, i_=i_: e.activation(
                        KTS[0][:, i_ * 512:i_ * 512 + 512], pb(bk, 0, 512), AF.Copy),
                        ([("T", t_s2)] + g_w("KTS0")) if i_ == 0 else [],
                        inc="S" if i_ == len(kt_bk) - 1 else None)
                s_s3 = cnt["S"]
                for bk in kt_bk:
                    g_set(f"bank{bk}", ("S", s_s3))
                for j in range(wavesz):
                    waits = (g_w(*[f"bank{bk}" for bk in s_bk]) + [("V", v_s1), ("S", s_s3)]) if j == 0 else []
                    sb_ = s_bk[j // 8] if wavesz == 16 else s_bk[0]
                    so = (j % 8) * 64
                    emit("tensor", lambda e, j=j, pw=pw, sb_=sb_, so=so: e.matmul(
                        pb(sb_, so, 64),
                        QBD[pw][:, j * 128:j * 128 + 128],
                        KTS[0][:, (j % 2) * 512 + (j // 2) * 64:(j % 2) * 512 + (j // 2) * 64 + 64],
                        start=True, stop=True),
                        waits, inc="T" if j == wavesz - 1 else None)
                t_s4 = cnt["T"]
                g_set("KTS0", ("T", t_s4))
                yield
                if wavesz == 16:
                    v_a = emit("vector", lambda e, sb_=s_bk[0]: e.reduce_max(
                        NM[:], pb(sb_, 0, 512), AX, negate=True),
                        [("T", t_s4)] + g_w("NM") + g_w("NM2"), inc="V")
                    v_b = emit("vector", lambda e, sb_=s_bk[1]: e.reduce_max(
                        NM2[:], pb(sb_, 0, 512), AX, negate=True), inc="V")
                    v_s5 = emit("vector", lambda e: e.tensor_tensor(NM[:], NM[:], NM2[:], ALU.min),
                                [("V", v_b)], inc="V")
                else:
                    v_s5 = emit("vector", lambda e, sb_=s_bk[0]: e.reduce_max(
                        NM[:], pb(sb_, 0, 512), AX, negate=True),
                        [("T", t_s4)] + g_w("NM"), inc="V")
                nbk = 2 if wavesz == 16 else 1
                for a in range(2):
                    for bk_i in range(nbk):
                        s_s6 = emit("scalar", lambda e, a=a, pw=pw, bk_i=bk_i, sb_=s_bk[bk_i]: e.activation(
                            PSTG[pw][64 * a:64 * a + 64, bk_i * 1024:bk_i * 1024 + 1024].rearrange("p (j x) -> p j x", x=128)[:, :, 64 * a:64 * a + 64],
                            PALL[64 * a:64 * a + 64, sb_ * 512:sb_ * 512 + 512].rearrange("p (j g) -> p j g", g=64),
                            AF.Exp, bias=NM[64 * a:64 * a + 64, 0:1]),
                            ([("V", v_s5)] + g_w(f"PSTG{pw}")) if (a == 0 and bk_i == 0) else [],
                            inc="S" if (a == 1 and bk_i == nbk - 1) else None)
                s_s6 = cnt["S"]
                g_set("NM", ("S", s_s6))
                for bk in s_bk:
                    g_set(f"bank{bk}", ("S", s_s6))
                nj = wavesz
                v_z = emit("vector", lambda e, pw=pw, nj=nj: e.reduce_sum(
                    ZW[:, 0:nj].rearrange("p (j o) -> p j o", o=1),
                    PSTG[pw][:, 0:nj * 128].rearrange("p (j x) -> p j x", x=128), AX),
                    [("S", s_s6)] + g_w("ZW"), inc="V")
                for j in range(wavesz):
                    waits = (g_w(*[f"bank{bk}" for bk in pt_bk]) + [("S", s_s6)]) if j == 0 else []
                    emit("tensor", lambda e, j=j, pw=pw, ptb=pt_bk[j // 4], po=(j % 4) * 128: e.matmul(
                        pb(ptb, po, 128), PSTG[pw][:, j * 128:j * 128 + 128],
                        idt[:], start=True, stop=True), waits, inc="T" if j == wavesz - 1 else None)
                t_s7 = cnt["T"]
                g_set(f"PSTG{pw}", ("T", t_s7), ("V", v_z))
                yield
                for i_, bk in enumerate(pt_bk):
                    emit("scalar", lambda e, pw=pw, bk=bk, i_=i_: e.activation(
                        QBD[pw][:, i_ * 512:i_ * 512 + 512], pb(bk, 0, 512),
                        AF.Copy), [("T", t_s7)] if i_ == 0 else [],
                        inc="S" if i_ == len(pt_bk) - 1 else None)
                s_s8 = cnt["S"]
                for bk in pt_bk:
                    g_set(f"bank{bk}", ("S", s_s8))
                for j in range(wavesz):
                    jj = w * wavesz + j
                    waits = (g_w(*[f"bank{bk}" for bk in o_bk]) + [("S", s_s8)]) if j == 0 else []
                    ob_ = o_bk[j // 8] if wavesz == 16 else o_bk[0]
                    oo = (j % 8) * 64
                    emit("tensor", lambda e, j=j, jj=jj, pw=pw, p=p, ob_=ob_, oo=oo: e.matmul(
                        pb(ob_, oo, 64),
                        QBD[pw][:, j * 128:j * 128 + 128],
                        VB[p][:, jj * 64:jj * 64 + 64], start=True, stop=True),
                        waits, inc="T" if j == wavesz - 1 else None)
                t_s9 = cnt["T"]
                g_set(f"QBD{pw}", ("T", t_s9))
                yield
                v_r = emit("vector", lambda e, nj=nj: e.reciprocal(RZW[:, 0:nj], ZW[:, 0:nj]),
                           [("V", v_z)] + g_w("RZW"), inc="V")
                g_set("ZW", ("V", v_r))
                wob = g_w(f"OB{p}") if w == 0 else []
                for bk_i, bk in enumerate(o_bk):
                    nper = 8
                    emit("vector", lambda e, w=w, p=p, bk=bk, bk_i=bk_i, wavesz=wavesz: e.tensor_tensor(
                        OB[p][:, w * wavesz * 64 + bk_i * 512:w * wavesz * 64 + bk_i * 512 + 512].rearrange("p (j d) -> p j d", d=64),
                        pb(bk, 0, 512).rearrange("p (j d) -> p j d", d=64),
                        RZW[:, bk_i * 8:bk_i * 8 + 8].rearrange("p (j o) -> p j o", o=1).broadcast_to([128, 8, 64]),
                        ALU.mult), ([("T", t_s9), ("V", v_r)] + wob) if bk_i == 0 else [],
                        inc="V" if bk_i == len(o_bk) - 1 else None)
                v_s10 = cnt["V"]
                g_set("RZW", ("V", v_s10))
                for bk in o_bk:
                    g_set(f"bank{bk}", ("V", v_s10))
                yield
            g_set(f"QSG{p}", ("T", t_s0))
            g_set(f"KB{p}", ("T", t_s2))
            g_set(f"VB{p}", ("T", t_s9))
            for a in range(2):
                qv = dma("sync",
                         on_d[base0 + 64 * a: base0 + 64 * a + 32, :].rearrange("t (h d) -> h t d", h=64),
                         OB[p][64 * a:64 * a + 64, :].rearrange("h (j d) -> h j d", d=64),
                         [("V", v_s10)], sem=f"ob{p}")
            g_set(f"OB{p}", (f"ob{p}", qv))
            yield

        def a3_units(taus, wavesz, banks):
            for tau in taus:
                for bt in range(2):
                    yield from attn_batch(tau, bt, wavesz, banks)
                on_q = [("ob0", cnt["ob0"]), ("ob1", cnt["ob1"])]
                uq = dma("sync", BFN[:], on_d[tau * 128:tau * 128 + 128, :],
                         list(on_q) + g_w("BFN"), sem="onr")
                lnt_key = "LNTa" if tau < 4 else "LNTb"
                fw = [("onr", uq)] + (g_w(lnt_key) if tau % 4 == 0 else [])
                yield
                transpose_tile(BFN, lambda kt, tau=tau: colk(kt, tau), fw, banks=banks)
                g_set("BFN", ("T", cnt["T"]))
                yield

        def drive(gen, k):
            for _ in range(k):
                if next(gen, "END") == "END":
                    return False
            return True

        def drain(gen):
            for _ in gen:
                pass

        ph(3, c)
        drain(a3_units(range(0, 4), 16, (0, 1, 2, 3, 4, 5, 6, 7)))
        a35h0_v, a35h0_s = cnt["V"], cnt["S"]

        ph(4, c)
        ap_i = [0]

        def proj_half(nm, taus, banks, src, gen, img_d, first_waits=()):
            for n in range(NB):
                for ktg in range(8):
                    b, rdy = load_wgroup(nm, n, ktg, src)
                    for k4 in range(4):
                        kt = ktg * 4 + k4
                        for ti, tau in enumerate(taus):
                            waits = []
                            if k4 == 0 and ti == 0:
                                waits.append(rdy)
                            if ktg == 0 and k4 == 0:
                                waits += g_w(f"bank{banks[ti]}")
                                if n == 0 and ti == 0:
                                    waits += list(first_waits)
                            emit("tensor",
                                 lambda e, kt=kt, tau=tau, b=b, k4=k4, bk=banks[ti],
                                        st=(ktg == 0 and k4 == 0), sp=(ktg == 7 and k4 == 3):
                                 e.matmul(pb(bk, 0, 512),
                                          LNT[:, colk(kt, tau):colk(kt, tau) + 128],
                                          WG[b][:, k4 * 512:k4 * 512 + 512], start=st, stop=sp),
                                 waits, inc="T" if (k4 == 3 and ti == len(taus) - 1) else None)
                    wg_done(b, "T", cnt["T"])
                    if gen is not None:
                        drive(gen, 3)
                tmm = cnt["T"]
                for ti, tau in enumerate(taus):
                    sl = ap_i[0] % 4
                    ap_i[0] += 1
                    ev = emit("scalar", lambda e, bk=banks[ti], sl=sl: e.activation(
                        APST[:, sl * 512:sl * 512 + 512], pb(bk, 0, 512), AF.Copy),
                        ([("T", tmm)] if ti == 0 else []) + g_w(f"APs{sl}"), inc="S")
                    g_set(f"bank{banks[ti]}", ("S", ev))
                    uv = dma("sync", img_d[tau * 128:tau * 128 + 128, 512 * n:512 * n + 512],
                             APST[:, sl * 512:sl * 512 + 512], [("S", ev)], sem=f"ap{sl}")
                    g_set(f"APs{sl}", (f"ap{sl}", uv))

        gen_h1 = a3_units(range(4, 8), 8, (4, 5, 6, 7))
        proj_half("wo", [0, 1, 2, 3], [0, 1, 2, 3], src_kind0, gen_h1, ap_d,
                  [("V", a35h0_v), ("S", a35h0_s)])
        drain(gen_h1)
        g_set("LNTa", ("T", cnt["T"]))
        a4h0_ap = [(f"ap{i}", cnt[f"ap{i}"]) for i in range(4)]
        a4h0_vs = [("V", cnt["V"]), ("S", cnt["S"])]

        def a5_b0_units(taus, tbank):
            for tau in taus:
                ap_waits = a4h0_ap if tau < 4 else a4h1_ap[0]
                uq1 = dma("gpsimd", T2[:], ap_d[tau * 128:tau * 128 + 128, :],
                          list(ap_waits) + g_w("T2"), sem="t2l")
                uq2 = dma("gpsimd", XT[:], x_d[row0 + tau * 128: row0 + tau * 128 + 128, :],
                          g_w("XT"), sem="xl")
                av = lnorm(T2[:], T2[:], [("t2l", uq1)])
                yield
                v_ = emit("vector", lambda e: e.tensor_add(XT[:], XT[:], T2[:]),
                          [("xl", uq2), ("V", av)], inc="V")
                g_set("T2", ("V", v_))
                uq3 = dma("gpsimd", xp_d[row0 + tau * 128: row0 + tau * 128 + 128, :], XT[:],
                          [("V", v_)], sem="xps")
                av = lnorm(XT[:], BFN[:], [("V", v_)] + g_w("BFN"))
                g_set("XT", ("V", av), ("xps", uq3))
                lnt_key = "LNTa" if tau < 4 else "LNTb"
                fw = [("V", av)] + (g_w(lnt_key) if tau % 4 == 0 else [])
                yield
                transpose_tile(BFN, lambda kt, tau=tau: colk(kt, tau), fw, banks=tbank)
                g_set("BFN", ("T", cnt["T"]))
                tr_v, tr_s = cnt["V"], cnt["S"]
                yield
                lbk = tbank[tau % 4]
                for kt in range(KT):
                    waits = []
                    if kt == 0:
                        waits = g_w(f"bank{lbk}") + [("setu", 16), ("V", tr_v), ("S", tr_s)]
                    emit("tensor", lambda e, kt=kt, tau=tau, lbk=lbk: e.matmul(
                        pb(lbk, 0, 8), LNT[:, colk(kt, tau):colk(kt, tau) + 128],
                        wrT_s[:, kt * 8:kt * 8 + 8], start=(kt == 0), stop=(kt == KT - 1)),
                        waits, inc="T" if kt == KT - 1 else None)
                t_lg = cnt["T"]
                yield
                v_ = emit("vector", lambda e, lbk=lbk: e.tensor_copy(LGT[:], pb(lbk, 0, 8)),
                          [("T", t_lg)] + g_w("LGT"), inc="V")
                v_ = emit("vector", lambda e: e.max(out=SRT[:], in_=LGT[:]), [("V", v_)], inc="V")
                v_ = emit("vector", lambda e: e.tensor_sub(CC[:, 0:1], SRT[:, 1:2], SRT[:, 0:1]),
                          [("V", v_)], inc="V")
                s_ = emit("scalar", lambda e: e.activation(CC[:, 1:2], CC[:, 0:1], AF.Exp),
                          [("V", v_)], inc="S")
                v_ = emit("vector", lambda e: e.tensor_scalar_add(CC[:, 2:3], CC[:, 1:2], 1.0),
                          [("S", s_)], inc="V")
                v_ = emit("vector", lambda e: e.reciprocal(CC[:, 3:4], CC[:, 2:3]), [("V", v_)], inc="V")
                v_ = emit("vector", lambda e: e.tensor_mul(CC[:, 4:5], CC[:, 1:2], CC[:, 3:4]),
                          [("V", v_)], inc="V")
                v_ = emit("vector", lambda e: e.tensor_scalar(CC[:, 5:6], CC[:, 4:5], -1.0, 1.0,
                                                              ALU.mult, ALU.add), [("V", v_)], inc="V")
                emit("vector", lambda e: e.tensor_scalar(EQ1[:], LGT[:], SRT[:, 0:1], None, ALU.is_equal))
                v_ = emit("vector", lambda e: e.tensor_scalar(EQ2[:], LGT[:], SRT[:, 1:2], None, ALU.is_equal),
                          inc="V")
                v_ = emit("vector", lambda e: e.tensor_scalar(WFV[:], EQ2[:], CC[:, 4:5], None, ALU.mult),
                          [("V", v_)], inc="V")
                v_ = emit("vector", lambda e: e.scalar_tensor_tensor(WFV[:], EQ1[:], CC[:, 5:6], WFV[:],
                                                                     ALU.mult, ALU.add), [("V", v_)], inc="V")
                v_ = emit("vector", lambda e: e.tensor_copy(WBF[:], WFV[:]), [("V", v_)], inc="V")
                g_set(f"bank{lbk}", ("V", v_))
                g_set("LGT", ("V", v_))
                yield
                bb_ = tbank[(tau + 1) % 4]
                t_ = emit("tensor", lambda e, bb_=bb_: e.matmul(
                    PALL[0:8, bb_ * 512:bb_ * 512 + 128], WBF[:], idt[:], start=True, stop=True),
                    [("V", v_)] + g_w(f"bank{bb_}"), inc="T")
                v_ = emit("vector", lambda e, tau=tau, bb_=bb_: e.tensor_copy(
                    WTS[:, tau * 128:tau * 128 + 128], PALL[0:8, bb_ * 512:bb_ * 512 + 128]),
                    [("T", t_)], inc="V")
                g_set(f"bank{bb_}", ("V", v_))
                yield

        gen_a5h0 = a5_b0_units(range(0, 4), (0, 1, 2, 3))
        a4h1_ap = [None]
        proj_half("wo", [4, 5, 6, 7], [4, 5, 6, 7], "scratch", gen_a5h0, ap_d, a4h0_vs)
        a4h1_ap[0] = [(f"ap{i}", cnt[f"ap{i}"]) for i in range(4)]
        drain(gen_a5h0)
        g_set("LNTb", ("T", cnt["T"]))
        ph(5, c)

        hwt_done = [None]

        def moe_tail_units():
            yield from a5_b0_units(range(4, 8), (4, 5, 6, 7))
            v_wts = cnt["V"]
            for th in range(2):
                t_ = emit("tensor", lambda e, th=th: e.matmul(
                    pb(4 + th, 0, 512), rep[:], WTS[:, th * 512:th * 512 + 512],
                    start=True, stop=True), [("V", v_wts), ("setq", 48)] + g_w(f"bank{4 + th}"), inc="T")
                v_ = emit("vector", lambda e, th=th: e.tensor_copy(
                    WEX[:, th * 512:th * 512 + 512], pb(4 + th, 0, 512)), [("T", t_)], inc="V")
                g_set(f"bank{4 + th}", ("V", v_))
            v_wex = cnt["V"]
            yield
            for ah in range(2):
                waits = g_w("WG0")
                src = aT_d.rearrange("(kt p) m -> p kt m", p=128)[:, ah * 16:ah * 16 + 16, :]
                uv = dma("gpsimd", WG[0][:].rearrange("p (kt m) -> p kt m", kt=16), src, waits,
                         sem="wgu0")
                for th in range(2):
                    for kt16 in range(16):
                        kt = ah * 16 + kt16
                        waits = []
                        if kt16 == 0 and th == 0:
                            waits = [("wgu0", uv)]
                            if ah == 0:
                                waits += g_w("bank6", "bank7")
                        emit("tensor",
                             lambda e, th=th, kt=kt, kt16=kt16,
                                    st=(ah == 0 and kt16 == 0), sp=(ah == 1 and kt16 == 15):
                             e.matmul(pb(6 + th, 0, 512), WG[0][:, kt16 * 128:kt16 * 128 + 128],
                                      LNT[:, kt * CH + th * 512: kt * CH + th * 512 + 512],
                                      start=st, stop=sp), waits,
                             inc="T" if (th == 1 and kt16 == 15) else None)
                g_set("WG0", ("T", cnt["T"]))
                yield
            t_ht = cnt["T"]
            for th in range(2):
                v_ = emit("vector", lambda e, th=th: e.tensor_tensor(
                    HWT[:, th * 512:th * 512 + 512], pb(6 + th, 0, 512),
                    WEX[:, th * 512:th * 512 + 512], ALU.mult),
                    [("T", t_ht), ("V", v_wex)], inc="V")
                g_set(f"bank{6 + th}", ("V", v_))
            hwt_done[0] = ("V", cnt["V"])
            yield

        ph(7, c)
        y_i = [0]

        def b1_half(taus, banks, src, gen):
            for n in range(NB):
                bfb = n % 2
                uv_bf = dma("gpsimd", bfTn[bfb][:], bfT_d[:, n * 512:n * 512 + 512],
                            g_w(f"bfTn{bfb}"), sem=f"bfl{bfb}")
                uv_bb = dma("gpsimd", bbn[bfb][:], bb_d[:, n * 512:n * 512 + 512],
                            g_w(f"bbn{bfb}"), sem=f"bbl{bfb}")
                for ktg in range(8):
                    b, rdy = load_wgroup("wb", n, ktg, src)
                    for k4 in range(4):
                        kt = ktg * 4 + k4
                        for ti, tau in enumerate(taus):
                            waits = []
                            if k4 == 0 and ti == 0:
                                waits.append(rdy)
                            if ktg == 0 and k4 == 0:
                                waits += g_w(f"bank{banks[ti]}")
                            emit("tensor",
                                 lambda e, kt=kt, tau=tau, b=b, k4=k4, bk=banks[ti],
                                        st=(ktg == 0 and k4 == 0):
                                 e.matmul(pb(bk, 0, 512),
                                          LNT[:, colk(kt, tau):colk(kt, tau) + 128],
                                          WG[b][:, k4 * 512:k4 * 512 + 512],
                                          start=st, stop=False), waits,
                                 inc="T" if (k4 == 3 and ti == len(taus) - 1) else None)
                    wg_done(b, "T", cnt["T"])
                    if gen is not None:
                        drive(gen, 3)
                if gen is not None:
                    drain(gen)
                for ti, tau in enumerate(taus):
                    waits = ([(f"bbl{bfb}", uv_bb)] if ti == 0 else []) + \
                            ([("G", g0)] if (n == 0 and ti == 0 and c == 0) else [])
                    emit("tensor", lambda e, bk=banks[ti], bfb=bfb: e.matmul(
                        pb(bk, 0, 512), onesb[:], bbn[bfb][:], start=False, stop=False),
                        waits)
                    waits = ([(f"bfl{bfb}", uv_bf), hwt_done[0]] if ti == 0 else [])
                    tmm = emit("tensor", lambda e, tau=tau, bfb=bfb, bk=banks[ti]: e.matmul(
                        pb(bk, 0, 512), HWT[:, tau * 128:tau * 128 + 128], bfTn[bfb][:],
                        start=False, stop=True), waits, inc="T")
                    sl = y_i[0] % 4
                    y_i[0] += 1
                    ev = emit("scalar", lambda e, bk=banks[ti], sl=sl: e.activation(
                        APST[:, sl * 512:sl * 512 + 512], pb(bk, 0, 512), AF.Copy),
                        [("T", tmm)] + g_w(f"APs{sl}"), inc="S")
                    g_set(f"bank{banks[ti]}", ("S", ev))
                    uv = dma("sync", yn_d[tau * 128:tau * 128 + 128, 512 * n:512 * n + 512],
                             APST[:, sl * 512:sl * 512 + 512], [("S", ev)], sem=f"ap{sl}")
                    g_set(f"APs{sl}", (f"ap{sl}", uv))
                g_set(f"bfTn{bfb}", ("T", cnt["T"]))
                g_set(f"bbn{bfb}", ("T", cnt["T"]))

        gen_moe = moe_tail_units()
        drive(gen_moe, 6)
        b1_half([0, 1, 2, 3], [0, 1, 2, 3], src_kind0, gen_moe)
        drain(gen_moe)
        b1h0_y = [(f"ap{i}", cnt[f"ap{i}"]) for i in range(4)]

        ph(8, c)
        def b6_units(taus, ywaits):
            for tau in taus:
                uq1 = dma("gpsimd", T2[:], yn_d[tau * 128:tau * 128 + 128, :],
                          list(ywaits) + g_w("T2"), sem="t2l")
                uq2 = dma("gpsimd", XT[:], xp_d[row0 + tau * 128: row0 + tau * 128 + 128, :],
                          g_w("XT"), sem="xl")
                av = lnorm(T2[:], T2[:], [("t2l", uq1)])
                v_ = emit("vector", lambda e: e.tensor_add(XT[:], XT[:], T2[:]),
                          [("xl", uq2), ("V", av)], inc="V")
                g_set("T2", ("V", v_))
                uq3 = dma("gpsimd", out_d[row0 + tau * 128: row0 + tau * 128 + 128, :], XT[:],
                          [("V", v_)], sem="outs")
                g_set("XT", ("outs", uq3))

        b6_units(range(0, 4), b1h0_y)
        b1_half([4, 5, 6, 7], [4, 5, 6, 7], "scratch", None)
        g_set("LNTa", ("T", cnt["T"]))
        g_set("LNTb", ("T", cnt["T"]))
        b1h1_y = [(f"ap{i}", cnt[f"ap{i}"]) for i in range(4)]
        b6_units(range(4, 8), b1h1_y)
      except _Stop:
        pass

    def runner(name):
        def run(eng):
            for waits, fn, inc, amt in sched[name]:
                for s_, v_ in waits:
                    eng.wait_ge(sems[s_], v_)
                instr = fn(eng)
                if inc is not None:
                    instr.then_inc(sems[inc], amt)
        return run

    blk.tensor(runner("tensor"))
    blk.vector(runner("vector"))
    blk.scalar(runner("scalar"))
    blk.gpsimd(runner("gpsimd"))
    blk.sync(runner("sync"))
    ctx.close()
    return nc


_NC_CACHE = [None]


def kernel(L0_attn_hidden_states, L0_attn_Wq, L0_attn_Wk, L0_attn_Wv, L0_attn_Wo,
           L0_attn_mu_q, L0_attn_sigma_q, L0_attn_mu_kv, L0_attn_sigma_kv,
           L0_attn_eps_q, L0_attn_eps_kv,
           L0_ffn_W_base, L0_ffn_b_base, L0_ffn_W_router, L0_ffn_A, L0_ffn_B):
    xf = np.ascontiguousarray(np.asarray(L0_attn_hidden_states, dtype=np.float32).reshape(TOK, DM))
    wqT = np.ascontiguousarray(np.asarray(L0_attn_Wq, np.float32).T)
    wkT = np.ascontiguousarray(np.asarray(L0_attn_Wk, np.float32).T)
    wvT = np.ascontiguousarray(np.asarray(L0_attn_Wv, np.float32).T)
    woT = np.ascontiguousarray(np.asarray(L0_attn_Wo, np.float32).T)
    wbT = np.ascontiguousarray(np.asarray(L0_ffn_W_base, np.float32).T)
    wrT = np.ascontiguousarray(np.asarray(L0_ffn_W_router, np.float32).T)
    aT = np.ascontiguousarray(np.asarray(L0_ffn_A, np.float32).reshape(128, DM).T)
    bfT = np.ascontiguousarray(np.asarray(L0_ffn_B, np.float32).transpose(0, 2, 1).reshape(128, DM))
    bb = np.ascontiguousarray(np.asarray(L0_ffn_b_base, np.float32).reshape(1, DM))
    smalls = np.ascontiguousarray(np.concatenate([
        np.asarray(a, np.float32).reshape(-1) for a in
        (L0_attn_mu_q, L0_attn_sigma_q, L0_attn_eps_q,
         L0_attn_mu_kv, L0_attn_sigma_kv, L0_attn_eps_kv)]).reshape(1, 384))

    if _NC_CACHE[0] is None:
        _NC_CACHE[0] = _build_program()
    nc = _NC_CACHE[0]

    in_maps = [{
        "x": xf[c * TPC:(c + 1) * TPC],
        "wqT": wqT, "wkT": wkT, "wvT": wvT, "woT": woT, "wbT": wbT,
        "wrT": wrT, "aT": aT, "bfT": bfT, "bb": bb, "smalls": smalls,
    } for c in range(NCORES)]
    res = run_bass_kernel_spmd(nc, in_maps, core_ids=list(range(NCORES)))
    out = np.concatenate([r["out"] for r in res.results], axis=0)
    return out.reshape(4, 4096, 4096)




# version 19
# speedup vs baseline: 1.0844x; 1.0844x over previous
import os
import numpy as np
import ml_dtypes
import concourse.bass as bass
import concourse.mybir as mybir
from concourse.bass_utils import run_bass_kernel_spmd
from contextlib import ExitStack

F32 = mybir.dt.float32
BF16 = mybir.dt.bfloat16
AX = mybir.AxisListType.X
AF = mybir.ActivationFunctionType
ALU = mybir.AluOpType

NCORES = 8
TOK = 16384
TPC = TOK // NCORES
CH = 1024
NCH = TPC // CH
TAU = CH // 128
KT = 32
NB = 8
DM = 4096
EPS = 1e-5


def _build_program():
    nc = bass.Bass("TRN2", target_bir_lowering=False,
                   detect_race_conditions=bool(os.environ.get("KRACE")))

    x_d = nc.dram_tensor("x", [TPC, DM], F32, kind="ExternalInput").ap()
    wT = {nm: nc.dram_tensor(nm + "T", [DM, DM], F32, kind="ExternalInput").ap()
          for nm in ("wq", "wk", "wv", "wo", "wb")}
    wrT_d = nc.dram_tensor("wrT", [DM, 8], F32, kind="ExternalInput").ap()
    aT_d = nc.dram_tensor("aT", [DM, 128], F32, kind="ExternalInput").ap()
    bfT_d = nc.dram_tensor("bfT", [128, DM], F32, kind="ExternalInput").ap()
    bb_d = nc.dram_tensor("bb", [1, DM], F32, kind="ExternalInput").ap()
    sm_d = nc.dram_tensor("smalls", [1, 384], F32, kind="ExternalInput").ap()
    out_d = nc.dram_tensor("out", [TPC, DM], F32, kind="ExternalOutput").ap()

    ident_d = nc.inline_tensor(np.eye(128, dtype=ml_dtypes.bfloat16), "identc").ap()
    REPc = np.zeros((8, 128), dtype=ml_dtypes.bfloat16)
    for e in range(8):
        REPc[e, 16 * e:16 * e + 16] = 2.0
    rep_d = nc.inline_tensor(REPc, "repc").ap()

    wS = {nm: nc.dram_tensor("s_" + nm, [NB, 8, 128, 2048], BF16, kind="Internal").ap()
          for nm in ("wq", "wk", "wv", "wo", "wb")}
    qn_d = nc.dram_tensor("qn", [CH, DM], BF16, kind="Internal").ap()
    kn_d = nc.dram_tensor("kn", [CH, DM], BF16, kind="Internal").ap()
    vn_d = nc.dram_tensor("vn", [CH, DM], BF16, kind="Internal").ap()
    on_d = nc.dram_tensor("on", [CH, DM], BF16, kind="Internal").ap()
    ap_d = nc.dram_tensor("apn", [CH, DM], F32, kind="Internal").ap()
    yn_d = nc.dram_tensor("yn", [CH, DM], F32, kind="Internal").ap()
    xp_d = nc.dram_tensor("xp", [TPC, DM], F32, kind="Internal").ap()

    ctx = ExitStack()
    sb = lambda n, sh, dt: ctx.enter_context(nc.sbuf_tensor(n, sh, dt))

    LNT = sb("LNT", [128, KT * 1024], BF16)
    WG = [sb(f"WG{i}", [128, 2048], BF16) for i in range(4)]
    QST = sb("QST", [128, 2048], BF16)
    APST = sb("APST", [128, 2048], F32)
    QSG = [sb(f"QSG{i}", [128, 4096], BF16) for i in range(2)]
    KB = [sb(f"KB{i}", [128, 2048], BF16) for i in range(2)]
    VB = [sb(f"VB{i}", [128, 2048], BF16) for i in range(2)]
    OB = [sb(f"OB{i}", [128, 2048], BF16) for i in range(2)]
    QBD = [sb(f"QBD{i}", [128, 2048], BF16) for i in range(2)]
    KTS = [sb("KTS0", [128, 1024], BF16)]
    PSTG = [sb(f"PSTG{i}", [128, 2048], BF16) for i in range(2)]
    XT = sb("XT", [128, DM], F32)
    T2 = sb("T2", [128, DM], F32)
    BFN = sb("BFN", [128, DM], BF16)
    idt = sb("idt", [128, 128], BF16)
    rep = sb("rep", [8, 128], BF16)
    onesb = sb("onesb", [1, 128], BF16)
    onesf = sb("onesf", [1, 128], F32)
    smw = sb("smw", [1, 384], F32)
    mrow = sb("mrow", [1, 128], F32)
    maskK = sb("maskK", [128, 64], F32)
    maskV = sb("maskV", [128, 64], F32)
    wrT_s = sb("wrT_s", [128, KT * 8], BF16)
    bbn = [sb(f"bbn{i}", [1, 512], BF16) for i in range(2)]
    bfTn = [sb(f"bfTn{i}", [128, 512], BF16) for i in range(2)]
    BN = sb("BN", [128, 48], F32)
    MV = sb("MV", [128, 2], F32)
    RSTD = sb("RSTD", [128, 1], F32)
    NMR = sb("NMR", [128, 1], F32)
    NM = sb("NM", [128, 1], F32)
    NM2 = sb("NM2", [128, 1], F32)
    EPSC = sb("EPSC", [128, 1], F32)
    NHALF = sb("NHALF", [128, 1], F32)
    ZW = sb("ZW", [128, 16], F32)
    RZW = sb("RZW", [128, 16], F32)
    LGT = sb("LGT", [128, 8], F32)
    SRT = sb("SRT", [128, 8], F32)
    EQ1 = sb("EQ1", [128, 8], F32)
    EQ2 = sb("EQ2", [128, 8], F32)
    WFV = sb("WFV", [128, 8], F32)
    WBF = sb("WBF", [128, 8], BF16)
    CC = sb("CC", [128, 8], F32)
    WTS = sb("WTS", [8, 1024], BF16)
    WEX = sb("WEX", [128, 1024], BF16)
    HWT = sb("HWT", [128, 1024], BF16)
    PALL = ctx.enter_context(nc.psum_tensor("PALL", [128, 4096], F32))

    DMASEMS = (["wg0", "wg1", "wg2", "wg3", "wgu0", "wgu1", "wgu2", "wgu3",
                "sw0", "sw1", "sw2", "sw3", "bbl0", "bbl1", "qs0", "qs1", "qs2", "qs3", "qsg0", "qsg1", "qsg2", "qsg3",
                "rb0", "rb1", "ob0", "ob1", "onr", "xl", "t2l", "xps", "outs",
                "ap0", "ap1", "ap2", "ap3", "setq", "setu", "bfl0", "bfl1"])
    sems = {k: ctx.enter_context(nc.semaphore("sem" + k)) for k in list("TVSG") + DMASEMS}
    import sys
    print("SBUF remaining/partition after allocs:", nc.sbuf_bytes_remaining, file=sys.stderr)
    blk = ctx.enter_context(nc.Block())

    sched = {k: [] for k in ("tensor", "vector", "scalar", "gpsimd", "sync")}
    cnt = {k: 0 for k in (list("TVSG") + DMASEMS)}
    seen = {e: {k: 0 for k in cnt} for e in sched}
    ESEM = {"tensor": "T", "vector": "V", "scalar": "S", "gpsimd": "G"}

    def emit(eng, fn, waits=(), inc=None, amt=1):
        w = []
        for s_, v_ in waits:
            assert v_ <= cnt[s_], f"{eng}: wait on future {s_}={v_}>{cnt[s_]}"
            if v_ > seen[eng][s_]:
                seen[eng][s_] = v_
                w.append((s_, v_))
        if inc is not None:
            cnt[inc] += amt
        sched[eng].append((tuple(w), fn, inc, amt))
        return cnt[inc] if inc is not None else None

    guards = {}

    def g_w(*keys):
        out = []
        for k in keys:
            out += guards.get(k, [])
        return out

    def g_set(key, *pairs):
        guards[key] = list(pairs)

    def g_add(key, *pairs):
        guards.setdefault(key, []).extend(pairs)

    def dma(eng, out, in_, waits=(), sem=None):
        return emit(eng, lambda e, o=out, i=in_: e.dma_start(out=o, in_=i),
                    waits, inc=sem, amt=16)

    pb = lambda b, off, w: PALL[:, b * 512 + off: b * 512 + off + w]

    dq0_ = dma("sync", idt[:], ident_d, sem="setq")
    dma("sync", rep[:], rep_d, sem="setq")
    dq_sm_ = dma("sync", smw[:], sm_d, sem="setq")
    dma("gpsimd", wrT_s[:].rearrange("p (kt e) -> p kt e", e=8),
        wrT_d.rearrange("(kt p) e -> p kt e", p=128), sem="setu")
    emit("gpsimd", lambda e: e.memset(onesb[:], 1.0), inc="G")
    emit("gpsimd", lambda e: e.memset(onesf[:], 1.0), inc="G")
    emit("gpsimd", lambda e: e.memset(PSTG[0][:], 0.0), inc="G")
    emit("gpsimd", lambda e: e.memset(PSTG[1][:], 0.0), inc="G")
    emit("gpsimd", lambda e: e.memset(QSG[0][:], 0.0), inc="G")
    emit("gpsimd", lambda e: e.memset(QSG[1][:], 0.0), inc="G")
    emit("gpsimd", lambda e: e.memset(EPSC[:], EPS), inc="G")
    g0 = emit("gpsimd", lambda e: e.memset(NHALF[:], -0.5), inc="G")

    v1 = emit("vector", lambda e: e.tensor_mul(mrow[:, 0:64], smw[:, 128:192], smw[:, 64:128]),
              [("setq", 48)], inc="V")
    v1 = emit("vector", lambda e: e.tensor_add(mrow[:, 0:64], mrow[:, 0:64], smw[:, 0:64]),
              [("V", v1)], inc="V")
    emit("vector", lambda e: e.tensor_mul(mrow[:, 64:128], smw[:, 320:384], smw[:, 256:320]),
         inc="V")
    v1 = emit("vector", lambda e: e.tensor_add(mrow[:, 64:128], mrow[:, 64:128], smw[:, 192:256]),
              [("V", cnt["V"])], inc="V")
    s1 = emit("scalar", lambda e: e.activation(mrow[:], mrow[:], AF.Sigmoid),
              [("V", v1)], inc="S")
    v1 = emit("vector", lambda e: e.tensor_mul(mrow[:, 0:64], mrow[:, 0:64], mrow[:, 64:128]),
              [("S", s1)], inc="V")
    v1 = emit("vector", lambda e: e.tensor_scalar_mul(mrow[:, 0:64], mrow[:, 0:64], 0.125),
              [("V", v1)], inc="V")
    emit("tensor", lambda e: e.matmul(PALL[:, 0:64], onesf[:], mrow[:, 0:64],
                                      start=True, stop=True), [("V", v1), ("setq", 48), ("G", g0)])
    t1 = emit("tensor", lambda e: e.matmul(PALL[:, 512:576], onesf[:], mrow[:, 64:128],
                                           start=True, stop=True), inc="T")
    emit("vector", lambda e: e.tensor_copy(maskK[:], PALL[:, 0:64]), [("T", t1)])
    v_mask = emit("vector", lambda e: e.tensor_copy(maskV[:], PALL[:, 512:576]), inc="V")
    for i in range(8):
        g_set(f"bank{i}", ("V", v_mask))

    def lnorm(src_ap, dst_ap, vwaits=()):
        for i in range(8):
            emit("vector", lambda e, i=i: e.bn_stats(BN[:, 6 * i:6 * i + 6],
                                                     src_ap[:, 512 * i:512 * i + 512]),
                 (list(vwaits) + g_w("MV") + g_w("BN")) if i == 0 else (),
                 inc="V" if i == 7 else None)
        v_bn = cnt["V"]
        va = emit("vector", lambda e: e.bn_aggr(MV[:], BN[:]), [("V", v_bn)], inc="V")
        s_ln = emit("scalar", lambda e: e.activation(RSTD[:], MV[:, 1:2], AF.Ln, bias=EPSC[:, 0:1]),
                    [("V", va)] + g_w("RSTD"), inc="S")
        s_ = emit("scalar", lambda e: e.activation(RSTD[:], RSTD[:], AF.Exp, scale=NHALF[:, 0:1]),
                  [("S", s_ln)], inc="S")
        v_n = emit("vector", lambda e: e.tensor_scalar_mul(NMR[:], MV[:, 0:1], -1.0),
                   [("S", s_)] + g_w("NMR"), inc="V")
        av = emit("vector", lambda e: e.tensor_scalar(dst_ap, src_ap, NMR[:, 0:1],
                                                      RSTD[:, 0:1], ALU.add, ALU.mult),
                  [("V", v_n)], inc="V")
        g_set("BN", ("V", va))
        g_set("MV", ("S", s_ln), ("V", v_n))
        g_set("RSTD", ("V", av))
        g_set("NMR", ("V", av))
        return av

    def transpose_tile(src_sb, dst_col, first_waits=()):
        for w in range(4):
            for i in range(8):
                kt = w * 8 + i
                waits = g_w(f"bank{i}") + (list(first_waits) if kt == 0 else [])
                emit("tensor", lambda e, i=i, kt=kt: e.matmul(
                    pb(i, 0, 128), src_sb[:, kt * 128:kt * 128 + 128], idt[:],
                    start=True, stop=True), waits, inc="T" if i == 7 else None)
            tv = cnt["T"]
            eng = "vector" if w % 2 == 0 else "scalar"
            for i in range(8):
                kt = w * 8 + i
                if eng == "vector":
                    fn = (lambda e, i=i, kt=kt: e.tensor_copy(
                        LNT[:, dst_col(kt):dst_col(kt) + 128], pb(i, 0, 128)))
                else:
                    fn = (lambda e, i=i, kt=kt: e.copy(
                        LNT[:, dst_col(kt):dst_col(kt) + 128], pb(i, 0, 128)))
                emit(eng, fn, [("T", tv)] if i == 0 else [],
                     inc=ESEM[eng] if i == 7 else None)
            ev = cnt[ESEM[eng]]
            for i in range(8):
                g_set(f"bank{i}", (ESEM[eng], ev))
        return cnt["V"], cnt["S"]

    wg_idx = [0]

    def load_wgroup(nm, n, ktg, kind):
        b = wg_idx[0] % 4
        wg_idx[0] += 1
        waits = g_w(f"WG{b}")
        if kind == "cast_save":
            src = wT[nm].rearrange("(a p) c -> a p c", p=128)[
                4 * ktg:4 * ktg + 4, :, 512 * n:512 * n + 512].transpose([1, 0, 2])
            uv = dma("gpsimd", WG[b][:].rearrange("p (a c) -> p a c", a=4), src, waits,
                     sem=f"wgu{b}")
            rdy = (f"wgu{b}", uv)
            qv = dma("sync", wS[nm][n, ktg], WG[b][:], [rdy], sem=f"sw{b}")
            g_set(f"WG{b}", (f"sw{b}", qv))
        else:
            rdy = (f"wg{b}", dma("sync", WG[b][:], wS[nm][n, ktg], waits, sem=f"wg{b}"))
            g_set(f"WG{b}")
        return b, rdy

    def wg_done(b, sem, val):
        g_add(f"WG{b}", (sem, val))

    KPH = int(os.environ.get("KPHASE", "99"))

    class _Stop(Exception):
        pass

    def ph(k, c):
        if c * 10 + k > KPH:
            raise _Stop()

    def transpose_tile(src_sb, dst_col, first_waits=(), banks=(0, 1, 2, 3, 4, 5, 6, 7)):
        nb = len(banks)
        nwav = 32 // (nb * 4)
        for w in range(nwav):
            for i_, bk in enumerate(banks):
                for off in range(4):
                    kt = (w * nb + i_) * 4 + off
                    waits = (g_w(f"bank{bk}") if off == 0 else []) + \
                            (list(first_waits) if (w == 0 and i_ == 0 and off == 0) else [])
                    emit("tensor", lambda e, bk=bk, kt=kt, off=off: e.matmul(
                        pb(bk, off * 128, 128), src_sb[:, kt * 128:kt * 128 + 128], idt[:],
                        start=True, stop=True), waits,
                        inc="T" if off == 3 else None)
                tv = cnt["T"]
                eng = "vector" if (w * nb + i_) % 2 == 0 else "scalar"
                kb = (w * nb + i_) * 4
                dst = LNT[:].rearrange("p (kt x) -> p kt x", x=1024)[
                    :, kb:kb + 4, dst_col(0) - 0:dst_col(0) + 128] if False else None
                base = dst_col(kb) - kb * 1024
                dap = LNT[:].rearrange("p (kt x) -> p kt x", x=1024)[:, kb:kb + 4, base:base + 128]
                if eng == "vector":
                    fn = (lambda e, bk=bk, dap=dap: e.tensor_copy(dap, pb(bk, 0, 512)))
                else:
                    fn = (lambda e, bk=bk, dap=dap: e.copy(dap, pb(bk, 0, 512)))
                ev = emit(eng, fn, [("T", tv)], inc=ESEM[eng])
                g_set(f"bank{bk}", (ESEM[eng], ev))
        return cnt["V"], cnt["S"]

    for c in range(NCH):
      try:
        row0 = c * CH
        colk = lambda kt, tau: kt * CH + tau * 128
        src_kind0 = "cast_save" if c == 0 else "scratch"

        ph(1, c)
        for tau in range(TAU):
            xbuf, bname = (XT, "XT") if tau % 2 == 0 else (T2, "T2")
            lw = g_w(bname)
            if c == 0 and tau == 0:
                lw += [("setq", 48), ("G", g0)]
            qsem = "xl" if tau % 2 == 0 else "t2l"
            uq = dma("gpsimd", xbuf[:], x_d[row0 + tau * 128: row0 + tau * 128 + 128, :], lw,
                     sem=qsem)
            av = lnorm(xbuf[:], BFN[:], [(qsem, uq)] + g_w("BFN"))
            g_set(bname, ("V", av))
            fw = [("V", av)] + (g_w("LNTa", "LNTb") if tau == 0 else [])
            transpose_tile(BFN, lambda kt, tau=tau: colk(kt, tau), fw)
            g_set("BFN", ("T", cnt["T"]))
        a1_v, a1_s = cnt["V"], cnt["S"]

        ph(2, c)
        img = {"wq": qn_d, "wk": kn_d, "wv": vn_d}
        mask = {"wq": None, "wk": maskK, "wv": maskV}
        qst_i = [0]
        for nm in ("wq", "wk", "wv"):
            for n in range(NB):
                for ktg in range(8):
                    b, rdy = load_wgroup(nm, n, ktg, src_kind0)
                    for k4 in range(4):
                        kt = ktg * 4 + k4
                        for tau in range(TAU):
                            waits = []
                            if k4 == 0 and tau == 0:
                                waits.append(rdy)
                            if ktg == 0 and k4 == 0:
                                waits += g_w(f"bank{tau}")
                                if nm == "wq" and n == 0 and tau == 0:
                                    waits += [("V", a1_v), ("S", a1_s)]
                            last = (ktg == 7 and k4 == 3)
                            emit("tensor",
                                 lambda e, kt=kt, tau=tau, b=b, k4=k4,
                                        st=(ktg == 0 and k4 == 0), sp=last:
                                 e.matmul(pb(tau, 0, 512),
                                          LNT[:, colk(kt, tau):colk(kt, tau) + 128],
                                          WG[b][:, k4 * 512:k4 * 512 + 512],
                                          start=st, stop=sp),
                                 waits, inc="T" if (k4 == 3 and tau == TAU - 1) else None)
                    wg_done(b, "T", cnt["T"])
                tmm = cnt["T"]
                for tau in range(TAU):
                    sl = qst_i[0] % 4
                    qst_i[0] += 1
                    waits = ([("T", tmm)] if tau == 0 else []) + g_w(f"QSTs{sl}")
                    if mask[nm] is None:
                        ev = emit("scalar", lambda e, tau=tau, sl=sl: e.activation(
                            QST[:, sl * 512:sl * 512 + 512], pb(tau, 0, 512), AF.Copy),
                            waits, inc="S")
                        esem = "S"
                    else:
                        mk = mask[nm]
                        ev = emit("vector", lambda e, tau=tau, sl=sl, mk=mk: e.tensor_tensor(
                            QST[:, sl * 512:sl * 512 + 512].rearrange("p (h d) -> p h d", d=64),
                            pb(tau, 0, 512).rearrange("p (h d) -> p h d", d=64),
                            mk[:].rearrange("p (o d) -> p o d", o=1).broadcast_to([128, 8, 64]),
                            ALU.mult), waits + [("V", v_mask)], inc="V")
                        esem = "V"
                    g_set(f"bank{tau}", (esem, ev))
                    qeng, qsn = ("sync", f"qs{sl}") if c == 0 else ("gpsimd", f"qsg{sl}")
                    qv = dma(qeng, img[nm][tau * 128:tau * 128 + 128, 512 * n:512 * n + 512],
                             QST[:, sl * 512:sl * 512 + 512], [(esem, ev)], sem=qsn)
                    g_set(f"QSTs{sl}", (qsn, qv))
        g_set("LNTa", ("T", cnt["T"]))
        g_set("LNTb", ("T", cnt["T"]))
        qpre = "qs" if c == 0 else "qsg"
        a2_img_q = [(f"{qpre}{i}", cnt[f"{qpre}{i}"]) for i in range(4)]

        def attn_batch(tau, bt, wavesz, banks):
            p = bt
            base0 = tau * 128 + bt * 32
            w_first = list(a2_img_q) + [("G", g0)]
            for a in range(2):
                dst = QSG[p][64 * a:64 * a + 64, :].rearrange("h (j x) -> h j x", x=128)[:, :, 64 * a:64 * a + 64]
                src = qn_d[base0 + 64 * a: base0 + 64 * a + 32, :].rearrange("t (h d) -> h t d", h=64)
                dma("gpsimd", dst, src, w_first + g_w(f"QSG{p}") if a == 0 else [], sem=f"rb{p}")
            for a in range(2):
                for jp in range(2):
                    dst = KB[p][64 * jp:64 * jp + 64, :].rearrange("g (J x) -> g J x", x=128)[:, :, 64 * a:64 * a + 64]
                    src = kn_d[base0 + 64 * a + jp: base0 + 64 * a + 32: 2, :].rearrange("t (g d) -> g t d", g=64)
                    dma("gpsimd", dst, src, g_w(f"KB{p}") if (a == 0 and jp == 0) else [], sem=f"rb{p}")
            for a in range(2):
                dst = VB[p][64 * a:64 * a + 64, :].rearrange("g (j d) -> g j d", d=64)
                src = vn_d[base0 + 64 * a: base0 + 64 * a + 32, :].rearrange("t (g d) -> g t d", g=64)
                dma("gpsimd", dst, src, g_w(f"VB{p}") if a == 0 else [], sem=f"rb{p}")
            rbq = (f"rb{p}", cnt[f"rb{p}"])
            yield
            nwav = 32 // wavesz
            t_s0f = None
            for w in range(nwav):
                pw = w % 2
                if wavesz == 16:
                    qb_bk = banks[0:4]; kt_bk = banks[4:6]; s_bk = banks[6:8]
                    pt_bk = banks[0:4]; o_bk = banks[4:6]
                else:
                    qb_bk = banks[0:2]; kt_bk = banks[2:4]; s_bk = [banks[0]]
                    pt_bk = banks[2:4]; o_bk = [banks[1]]
                qpb = 4
                for j in range(wavesz):
                    jj = w * wavesz + j
                    waits = ([rbq] + g_w(*[f"bank{bk}" for bk in qb_bk])
                             + g_w(f"QBD{pw}")) if j == 0 else []
                    emit("tensor", lambda e, j=j, jj=jj, p=p, qb_bk=qb_bk: e.matmul(
                        pb(qb_bk[j // 4], (j % 4) * 128, 128), QSG[p][:, jj * 128:jj * 128 + 128],
                        idt[:], start=True, stop=True), waits,
                        inc="T" if j == wavesz - 1 else None)
                t_s0 = cnt["T"]
                if t_s0f is None:
                    t_s0f = t_s0
                yield
                for i_, bk in enumerate(qb_bk):
                    emit("vector", lambda e, pw=pw, bk=bk, i_=i_: e.tensor_copy(
                        QBD[pw][:, i_ * 512:i_ * 512 + 512], pb(bk, 0, 512)),
                        [("T", t_s0)] if i_ == 0 else [], inc="V" if i_ == len(qb_bk) - 1 else None)
                v_s1 = cnt["V"]
                for bk in qb_bk:
                    g_set(f"bank{bk}", ("V", v_s1))
                for j in range(wavesz):
                    jj = w * wavesz + j
                    waits = (g_w(*[f"bank{bk}" for bk in kt_bk]) + [rbq]) if j == 0 else []
                    ko_bk = kt_bk[jj % 2]
                    ko = (j // 2) * 64
                    emit("tensor", lambda e, j=j, jj=jj, p=p, ko_bk=ko_bk, ko=ko: e.matmul(
                        pb(ko_bk, ko, 64),
                        KB[p][64 * (jj % 2):64 * (jj % 2) + 64, (jj // 2) * 128:(jj // 2) * 128 + 128],
                        idt[64 * (jj % 2):64 * (jj % 2) + 64, 64 * (jj % 2):64 * (jj % 2) + 64],
                        start=True, stop=True), waits, inc="T" if j == wavesz - 1 else None)
                t_s2 = cnt["T"]
                yield
                for i_, bk in enumerate(kt_bk):
                    emit("scalar", lambda e, bk=bk, i_=i_: e.activation(
                        KTS[0][:, i_ * 512:i_ * 512 + 512], pb(bk, 0, 512), AF.Copy),
                        ([("T", t_s2)] + g_w("KTS0")) if i_ == 0 else [],
                        inc="S" if i_ == len(kt_bk) - 1 else None)
                s_s3 = cnt["S"]
                for bk in kt_bk:
                    g_set(f"bank{bk}", ("S", s_s3))
                for j in range(wavesz):
                    waits = (g_w(*[f"bank{bk}" for bk in s_bk]) + [("V", v_s1), ("S", s_s3)]) if j == 0 else []
                    sb_ = s_bk[j // 8] if wavesz == 16 else s_bk[0]
                    so = (j % 8) * 64
                    emit("tensor", lambda e, j=j, pw=pw, sb_=sb_, so=so: e.matmul(
                        pb(sb_, so, 64),
                        QBD[pw][:, j * 128:j * 128 + 128],
                        KTS[0][:, (j % 2) * 512 + (j // 2) * 64:(j % 2) * 512 + (j // 2) * 64 + 64],
                        start=True, stop=True),
                        waits, inc="T" if j == wavesz - 1 else None)
                t_s4 = cnt["T"]
                g_set("KTS0", ("T", t_s4))
                yield
                if wavesz == 16:
                    v_a = emit("vector", lambda e, sb_=s_bk[0]: e.reduce_max(
                        NM[:], pb(sb_, 0, 512), AX, negate=True),
                        [("T", t_s4)] + g_w("NM") + g_w("NM2"), inc="V")
                    v_b = emit("vector", lambda e, sb_=s_bk[1]: e.reduce_max(
                        NM2[:], pb(sb_, 0, 512), AX, negate=True), inc="V")
                    v_s5 = emit("vector", lambda e: e.tensor_tensor(NM[:], NM[:], NM2[:], ALU.min),
                                [("V", v_b)], inc="V")
                else:
                    v_s5 = emit("vector", lambda e, sb_=s_bk[0]: e.reduce_max(
                        NM[:], pb(sb_, 0, 512), AX, negate=True),
                        [("T", t_s4)] + g_w("NM"), inc="V")
                nbk = 2 if wavesz == 16 else 1
                for a in range(2):
                    for bk_i in range(nbk):
                        s_s6 = emit("scalar", lambda e, a=a, pw=pw, bk_i=bk_i, sb_=s_bk[bk_i]: e.activation(
                            PSTG[pw][64 * a:64 * a + 64, bk_i * 1024:bk_i * 1024 + 1024].rearrange("p (j x) -> p j x", x=128)[:, :, 64 * a:64 * a + 64],
                            PALL[64 * a:64 * a + 64, sb_ * 512:sb_ * 512 + 512].rearrange("p (j g) -> p j g", g=64),
                            AF.Exp, bias=NM[64 * a:64 * a + 64, 0:1]),
                            ([("V", v_s5)] + g_w(f"PSTG{pw}")) if (a == 0 and bk_i == 0) else [],
                            inc="S" if (a == 1 and bk_i == nbk - 1) else None)
                s_s6 = cnt["S"]
                g_set("NM", ("S", s_s6))
                for bk in s_bk:
                    g_set(f"bank{bk}", ("S", s_s6))
                nj = wavesz
                v_z = emit("vector", lambda e, pw=pw, nj=nj: e.reduce_sum(
                    ZW[:, 0:nj].rearrange("p (j o) -> p j o", o=1),
                    PSTG[pw][:, 0:nj * 128].rearrange("p (j x) -> p j x", x=128), AX),
                    [("S", s_s6)] + g_w("ZW"), inc="V")
                for j in range(wavesz):
                    waits = (g_w(*[f"bank{bk}" for bk in pt_bk]) + [("S", s_s6)]) if j == 0 else []
                    emit("tensor", lambda e, j=j, pw=pw, ptb=pt_bk[j // 4], po=(j % 4) * 128: e.matmul(
                        pb(ptb, po, 128), PSTG[pw][:, j * 128:j * 128 + 128],
                        idt[:], start=True, stop=True), waits, inc="T" if j == wavesz - 1 else None)
                t_s7 = cnt["T"]
                g_set(f"PSTG{pw}", ("T", t_s7), ("V", v_z))
                yield
                for i_, bk in enumerate(pt_bk):
                    emit("scalar", lambda e, pw=pw, bk=bk, i_=i_: e.activation(
                        QBD[pw][:, i_ * 512:i_ * 512 + 512], pb(bk, 0, 512),
                        AF.Copy), [("T", t_s7)] if i_ == 0 else [],
                        inc="S" if i_ == len(pt_bk) - 1 else None)
                s_s8 = cnt["S"]
                for bk in pt_bk:
                    g_set(f"bank{bk}", ("S", s_s8))
                for j in range(wavesz):
                    jj = w * wavesz + j
                    waits = (g_w(*[f"bank{bk}" for bk in o_bk]) + [("S", s_s8)]) if j == 0 else []
                    ob_ = o_bk[j // 8] if wavesz == 16 else o_bk[0]
                    oo = (j % 8) * 64
                    emit("tensor", lambda e, j=j, jj=jj, pw=pw, p=p, ob_=ob_, oo=oo: e.matmul(
                        pb(ob_, oo, 64),
                        QBD[pw][:, j * 128:j * 128 + 128],
                        VB[p][:, jj * 64:jj * 64 + 64], start=True, stop=True),
                        waits, inc="T" if j == wavesz - 1 else None)
                t_s9 = cnt["T"]
                g_set(f"QBD{pw}", ("T", t_s9))
                yield
                v_r = emit("vector", lambda e, nj=nj: e.reciprocal(RZW[:, 0:nj], ZW[:, 0:nj]),
                           [("V", v_z)] + g_w("RZW"), inc="V")
                g_set("ZW", ("V", v_r))
                wob = g_w(f"OB{p}") if w == 0 else []
                for bk_i, bk in enumerate(o_bk):
                    nper = 8
                    emit("vector", lambda e, w=w, p=p, bk=bk, bk_i=bk_i, wavesz=wavesz: e.tensor_tensor(
                        OB[p][:, w * wavesz * 64 + bk_i * 512:w * wavesz * 64 + bk_i * 512 + 512].rearrange("p (j d) -> p j d", d=64),
                        pb(bk, 0, 512).rearrange("p (j d) -> p j d", d=64),
                        RZW[:, bk_i * 8:bk_i * 8 + 8].rearrange("p (j o) -> p j o", o=1).broadcast_to([128, 8, 64]),
                        ALU.mult), ([("T", t_s9), ("V", v_r)] + wob) if bk_i == 0 else [],
                        inc="V" if bk_i == len(o_bk) - 1 else None)
                v_s10 = cnt["V"]
                g_set("RZW", ("V", v_s10))
                for bk in o_bk:
                    g_set(f"bank{bk}", ("V", v_s10))
                yield
            g_set(f"QSG{p}", ("T", t_s0))
            g_set(f"KB{p}", ("T", t_s2))
            g_set(f"VB{p}", ("T", t_s9))
            for a in range(2):
                qv = dma("sync",
                         on_d[base0 + 64 * a: base0 + 64 * a + 32, :].rearrange("t (h d) -> h t d", h=64),
                         OB[p][64 * a:64 * a + 64, :].rearrange("h (j d) -> h j d", d=64),
                         [("V", v_s10)], sem=f"ob{p}")
            g_set(f"OB{p}", (f"ob{p}", qv))
            yield

        def a3_units(taus, wavesz, banks):
            for tau in taus:
                for bt in range(2):
                    yield from attn_batch(tau, bt, wavesz, banks)
                on_q = [("ob0", cnt["ob0"]), ("ob1", cnt["ob1"])]
                uq = dma("gpsimd", BFN[:], on_d[tau * 128:tau * 128 + 128, :],
                         list(on_q) + g_w("BFN"), sem="onr")
                lnt_key = "LNTa" if tau < 4 else "LNTb"
                fw = [("onr", uq)] + (g_w(lnt_key) if tau % 4 == 0 else [])
                yield
                transpose_tile(BFN, lambda kt, tau=tau: colk(kt, tau), fw, banks=banks)
                g_set("BFN", ("T", cnt["T"]))
                yield

        def drive(gen, k):
            for _ in range(k):
                if next(gen, "END") == "END":
                    return False
            return True

        def drain(gen):
            for _ in gen:
                pass

        ph(3, c)
        drain(a3_units(range(0, 4), 16, (0, 1, 2, 3, 4, 5, 6, 7)))
        a35h0_v, a35h0_s = cnt["V"], cnt["S"]

        ph(4, c)
        ap_i = [0]

        def proj_half(nm, taus, banks, src, gen, img_d, first_waits=()):
            for n in range(NB):
                for ktg in range(8):
                    b, rdy = load_wgroup(nm, n, ktg, src)
                    for k4 in range(4):
                        kt = ktg * 4 + k4
                        for ti, tau in enumerate(taus):
                            waits = []
                            if k4 == 0 and ti == 0:
                                waits.append(rdy)
                            if ktg == 0 and k4 == 0:
                                waits += g_w(f"bank{banks[ti]}")
                                if n == 0 and ti == 0:
                                    waits += list(first_waits)
                            emit("tensor",
                                 lambda e, kt=kt, tau=tau, b=b, k4=k4, bk=banks[ti],
                                        st=(ktg == 0 and k4 == 0), sp=(ktg == 7 and k4 == 3):
                                 e.matmul(pb(bk, 0, 512),
                                          LNT[:, colk(kt, tau):colk(kt, tau) + 128],
                                          WG[b][:, k4 * 512:k4 * 512 + 512], start=st, stop=sp),
                                 waits, inc="T" if (k4 == 3 and ti == len(taus) - 1) else None)
                    wg_done(b, "T", cnt["T"])
                    if gen is not None:
                        drive(gen, 3)
                tmm = cnt["T"]
                for ti, tau in enumerate(taus):
                    sl = ap_i[0] % 4
                    ap_i[0] += 1
                    ev = emit("scalar", lambda e, bk=banks[ti], sl=sl: e.activation(
                        APST[:, sl * 512:sl * 512 + 512], pb(bk, 0, 512), AF.Copy),
                        ([("T", tmm)] if ti == 0 else []) + g_w(f"APs{sl}"), inc="S")
                    g_set(f"bank{banks[ti]}", ("S", ev))
                    uv = dma("gpsimd", img_d[tau * 128:tau * 128 + 128, 512 * n:512 * n + 512],
                             APST[:, sl * 512:sl * 512 + 512], [("S", ev)], sem=f"ap{sl}")
                    g_set(f"APs{sl}", (f"ap{sl}", uv))

        gen_h1 = a3_units(range(4, 8), 8, (4, 5, 6, 7))
        proj_half("wo", [0, 1, 2, 3], [0, 1, 2, 3], src_kind0, gen_h1, ap_d,
                  [("V", a35h0_v), ("S", a35h0_s)])
        drain(gen_h1)
        g_set("LNTa", ("T", cnt["T"]))
        a4h0_ap = [(f"ap{i}", cnt[f"ap{i}"]) for i in range(4)]
        a4h0_vs = [("V", cnt["V"]), ("S", cnt["S"])]

        def a5_b0_units(taus, tbank):
            for tau in taus:
                ap_waits = a4h0_ap if tau < 4 else a4h1_ap[0]
                uq1 = dma("gpsimd", T2[:], ap_d[tau * 128:tau * 128 + 128, :],
                          list(ap_waits) + g_w("T2"), sem="t2l")
                uq2 = dma("gpsimd", XT[:], x_d[row0 + tau * 128: row0 + tau * 128 + 128, :],
                          g_w("XT"), sem="xl")
                av = lnorm(T2[:], T2[:], [("t2l", uq1)])
                yield
                v_ = emit("vector", lambda e: e.tensor_add(XT[:], XT[:], T2[:]),
                          [("xl", uq2), ("V", av)], inc="V")
                g_set("T2", ("V", v_))
                uq3 = dma("gpsimd", xp_d[row0 + tau * 128: row0 + tau * 128 + 128, :], XT[:],
                          [("V", v_)], sem="xps")
                av = lnorm(XT[:], BFN[:], [("V", v_)] + g_w("BFN"))
                g_set("XT", ("V", av), ("xps", uq3))
                lnt_key = "LNTa" if tau < 4 else "LNTb"
                fw = [("V", av)] + (g_w(lnt_key) if tau % 4 == 0 else [])
                yield
                transpose_tile(BFN, lambda kt, tau=tau: colk(kt, tau), fw, banks=tbank)
                g_set("BFN", ("T", cnt["T"]))
                tr_v, tr_s = cnt["V"], cnt["S"]
                yield
                lbk = tbank[tau % 4]
                for kt in range(KT):
                    waits = []
                    if kt == 0:
                        waits = g_w(f"bank{lbk}") + [("setu", 16), ("V", tr_v), ("S", tr_s)]
                    emit("tensor", lambda e, kt=kt, tau=tau, lbk=lbk: e.matmul(
                        pb(lbk, 0, 8), LNT[:, colk(kt, tau):colk(kt, tau) + 128],
                        wrT_s[:, kt * 8:kt * 8 + 8], start=(kt == 0), stop=(kt == KT - 1)),
                        waits, inc="T" if kt == KT - 1 else None)
                t_lg = cnt["T"]
                yield
                v_ = emit("vector", lambda e, lbk=lbk: e.tensor_copy(LGT[:], pb(lbk, 0, 8)),
                          [("T", t_lg)] + g_w("LGT"), inc="V")
                v_ = emit("vector", lambda e: e.max(out=SRT[:], in_=LGT[:]), [("V", v_)], inc="V")
                v_ = emit("vector", lambda e: e.tensor_sub(CC[:, 0:1], SRT[:, 1:2], SRT[:, 0:1]),
                          [("V", v_)], inc="V")
                s_ = emit("scalar", lambda e: e.activation(CC[:, 1:2], CC[:, 0:1], AF.Exp),
                          [("V", v_)], inc="S")
                v_ = emit("vector", lambda e: e.tensor_scalar_add(CC[:, 2:3], CC[:, 1:2], 1.0),
                          [("S", s_)], inc="V")
                v_ = emit("vector", lambda e: e.reciprocal(CC[:, 3:4], CC[:, 2:3]), [("V", v_)], inc="V")
                v_ = emit("vector", lambda e: e.tensor_mul(CC[:, 4:5], CC[:, 1:2], CC[:, 3:4]),
                          [("V", v_)], inc="V")
                v_ = emit("vector", lambda e: e.tensor_scalar(CC[:, 5:6], CC[:, 4:5], -1.0, 1.0,
                                                              ALU.mult, ALU.add), [("V", v_)], inc="V")
                emit("vector", lambda e: e.tensor_scalar(EQ1[:], LGT[:], SRT[:, 0:1], None, ALU.is_equal))
                v_ = emit("vector", lambda e: e.tensor_scalar(EQ2[:], LGT[:], SRT[:, 1:2], None, ALU.is_equal),
                          inc="V")
                v_ = emit("vector", lambda e: e.tensor_scalar(WFV[:], EQ2[:], CC[:, 4:5], None, ALU.mult),
                          [("V", v_)], inc="V")
                v_ = emit("vector", lambda e: e.scalar_tensor_tensor(WFV[:], EQ1[:], CC[:, 5:6], WFV[:],
                                                                     ALU.mult, ALU.add), [("V", v_)], inc="V")
                v_ = emit("vector", lambda e: e.tensor_copy(WBF[:], WFV[:]), [("V", v_)], inc="V")
                g_set(f"bank{lbk}", ("V", v_))
                g_set("LGT", ("V", v_))
                yield
                bb_ = tbank[(tau + 1) % 4]
                t_ = emit("tensor", lambda e, bb_=bb_: e.matmul(
                    PALL[0:8, bb_ * 512:bb_ * 512 + 128], WBF[:], idt[:], start=True, stop=True),
                    [("V", v_)] + g_w(f"bank{bb_}"), inc="T")
                v_ = emit("vector", lambda e, tau=tau, bb_=bb_: e.tensor_copy(
                    WTS[:, tau * 128:tau * 128 + 128], PALL[0:8, bb_ * 512:bb_ * 512 + 128]),
                    [("T", t_)], inc="V")
                g_set(f"bank{bb_}", ("V", v_))
                yield

        gen_a5h0 = a5_b0_units(range(0, 4), (0, 1, 2, 3))
        a4h1_ap = [None]
        proj_half("wo", [4, 5, 6, 7], [4, 5, 6, 7], "scratch", gen_a5h0, ap_d, a4h0_vs)
        a4h1_ap[0] = [(f"ap{i}", cnt[f"ap{i}"]) for i in range(4)]
        drain(gen_a5h0)
        g_set("LNTb", ("T", cnt["T"]))
        ph(5, c)

        hwt_done = [None]

        def moe_tail_units():
            yield from a5_b0_units(range(4, 8), (4, 5, 6, 7))
            v_wts = cnt["V"]
            for th in range(2):
                t_ = emit("tensor", lambda e, th=th: e.matmul(
                    pb(4 + th, 0, 512), rep[:], WTS[:, th * 512:th * 512 + 512],
                    start=True, stop=True), [("V", v_wts), ("setq", 48)] + g_w(f"bank{4 + th}"), inc="T")
                v_ = emit("vector", lambda e, th=th: e.tensor_copy(
                    WEX[:, th * 512:th * 512 + 512], pb(4 + th, 0, 512)), [("T", t_)], inc="V")
                g_set(f"bank{4 + th}", ("V", v_))
            v_wex = cnt["V"]
            yield
            for ah in range(2):
                waits = g_w("WG0")
                src = aT_d.rearrange("(kt p) m -> p kt m", p=128)[:, ah * 16:ah * 16 + 16, :]
                uv = dma("gpsimd", WG[0][:].rearrange("p (kt m) -> p kt m", kt=16), src, waits,
                         sem="wgu0")
                for th in range(2):
                    for kt16 in range(16):
                        kt = ah * 16 + kt16
                        waits = []
                        if kt16 == 0 and th == 0:
                            waits = [("wgu0", uv)]
                            if ah == 0:
                                waits += g_w("bank6", "bank7")
                        emit("tensor",
                             lambda e, th=th, kt=kt, kt16=kt16,
                                    st=(ah == 0 and kt16 == 0), sp=(ah == 1 and kt16 == 15):
                             e.matmul(pb(6 + th, 0, 512), WG[0][:, kt16 * 128:kt16 * 128 + 128],
                                      LNT[:, kt * CH + th * 512: kt * CH + th * 512 + 512],
                                      start=st, stop=sp), waits,
                             inc="T" if (th == 1 and kt16 == 15) else None)
                g_set("WG0", ("T", cnt["T"]))
                yield
            t_ht = cnt["T"]
            for th in range(2):
                v_ = emit("vector", lambda e, th=th: e.tensor_tensor(
                    HWT[:, th * 512:th * 512 + 512], pb(6 + th, 0, 512),
                    WEX[:, th * 512:th * 512 + 512], ALU.mult),
                    [("T", t_ht), ("V", v_wex)], inc="V")
                g_set(f"bank{6 + th}", ("V", v_))
            hwt_done[0] = ("V", cnt["V"])
            yield

        ph(7, c)
        y_i = [0]

        def b1_half(taus, banks, src, gen):
            for n in range(NB):
                bfb = n % 2
                uv_bf = dma("gpsimd", bfTn[bfb][:], bfT_d[:, n * 512:n * 512 + 512],
                            g_w(f"bfTn{bfb}"), sem=f"bfl{bfb}")
                uv_bb = dma("gpsimd", bbn[bfb][:], bb_d[:, n * 512:n * 512 + 512],
                            g_w(f"bbn{bfb}"), sem=f"bbl{bfb}")
                for ktg in range(8):
                    b, rdy = load_wgroup("wb", n, ktg, src)
                    for k4 in range(4):
                        kt = ktg * 4 + k4
                        for ti, tau in enumerate(taus):
                            waits = []
                            if k4 == 0 and ti == 0:
                                waits.append(rdy)
                            if ktg == 0 and k4 == 0:
                                waits += g_w(f"bank{banks[ti]}")
                            emit("tensor",
                                 lambda e, kt=kt, tau=tau, b=b, k4=k4, bk=banks[ti],
                                        st=(ktg == 0 and k4 == 0):
                                 e.matmul(pb(bk, 0, 512),
                                          LNT[:, colk(kt, tau):colk(kt, tau) + 128],
                                          WG[b][:, k4 * 512:k4 * 512 + 512],
                                          start=st, stop=False), waits,
                                 inc="T" if (k4 == 3 and ti == len(taus) - 1) else None)
                    wg_done(b, "T", cnt["T"])
                    if gen is not None:
                        drive(gen, 3)
                if gen is not None:
                    drain(gen)
                for ti, tau in enumerate(taus):
                    waits = ([(f"bbl{bfb}", uv_bb)] if ti == 0 else []) + \
                            ([("G", g0)] if (n == 0 and ti == 0 and c == 0) else [])
                    emit("tensor", lambda e, bk=banks[ti], bfb=bfb: e.matmul(
                        pb(bk, 0, 512), onesb[:], bbn[bfb][:], start=False, stop=False),
                        waits)
                    waits = ([(f"bfl{bfb}", uv_bf), hwt_done[0]] if ti == 0 else [])
                    tmm = emit("tensor", lambda e, tau=tau, bfb=bfb, bk=banks[ti]: e.matmul(
                        pb(bk, 0, 512), HWT[:, tau * 128:tau * 128 + 128], bfTn[bfb][:],
                        start=False, stop=True), waits, inc="T")
                    sl = y_i[0] % 4
                    y_i[0] += 1
                    ev = emit("scalar", lambda e, bk=banks[ti], sl=sl: e.activation(
                        APST[:, sl * 512:sl * 512 + 512], pb(bk, 0, 512), AF.Copy),
                        [("T", tmm)] + g_w(f"APs{sl}"), inc="S")
                    g_set(f"bank{banks[ti]}", ("S", ev))
                    uv = dma("gpsimd", yn_d[tau * 128:tau * 128 + 128, 512 * n:512 * n + 512],
                             APST[:, sl * 512:sl * 512 + 512], [("S", ev)], sem=f"ap{sl}")
                    g_set(f"APs{sl}", (f"ap{sl}", uv))
                g_set(f"bfTn{bfb}", ("T", cnt["T"]))
                g_set(f"bbn{bfb}", ("T", cnt["T"]))

        gen_moe = moe_tail_units()
        drive(gen_moe, 6)
        b1_half([0, 1, 2, 3], [0, 1, 2, 3], src_kind0, gen_moe)
        drain(gen_moe)
        b1h0_y = [(f"ap{i}", cnt[f"ap{i}"]) for i in range(4)]

        ph(8, c)
        def b6_units(taus, ywaits):
            for tau in taus:
                uq1 = dma("gpsimd", T2[:], yn_d[tau * 128:tau * 128 + 128, :],
                          list(ywaits) + g_w("T2"), sem="t2l")
                uq2 = dma("gpsimd", XT[:], xp_d[row0 + tau * 128: row0 + tau * 128 + 128, :],
                          g_w("XT"), sem="xl")
                av = lnorm(T2[:], T2[:], [("t2l", uq1)])
                v_ = emit("vector", lambda e: e.tensor_add(XT[:], XT[:], T2[:]),
                          [("xl", uq2), ("V", av)], inc="V")
                g_set("T2", ("V", v_))
                uq3 = dma("gpsimd", out_d[row0 + tau * 128: row0 + tau * 128 + 128, :], XT[:],
                          [("V", v_)], sem="outs")
                g_set("XT", ("outs", uq3))

        b6_units(range(0, 4), b1h0_y)
        b1_half([4, 5, 6, 7], [4, 5, 6, 7], "scratch", None)
        g_set("LNTa", ("T", cnt["T"]))
        g_set("LNTb", ("T", cnt["T"]))
        b1h1_y = [(f"ap{i}", cnt[f"ap{i}"]) for i in range(4)]
        b6_units(range(4, 8), b1h1_y)
      except _Stop:
        pass

    def runner(name):
        def run(eng):
            for waits, fn, inc, amt in sched[name]:
                for s_, v_ in waits:
                    eng.wait_ge(sems[s_], v_)
                instr = fn(eng)
                if inc is not None:
                    instr.then_inc(sems[inc], amt)
        return run

    blk.tensor(runner("tensor"))
    blk.vector(runner("vector"))
    blk.scalar(runner("scalar"))
    blk.gpsimd(runner("gpsimd"))
    blk.sync(runner("sync"))
    ctx.close()
    return nc


_NC_CACHE = [None]


def kernel(L0_attn_hidden_states, L0_attn_Wq, L0_attn_Wk, L0_attn_Wv, L0_attn_Wo,
           L0_attn_mu_q, L0_attn_sigma_q, L0_attn_mu_kv, L0_attn_sigma_kv,
           L0_attn_eps_q, L0_attn_eps_kv,
           L0_ffn_W_base, L0_ffn_b_base, L0_ffn_W_router, L0_ffn_A, L0_ffn_B):
    xf = np.ascontiguousarray(np.asarray(L0_attn_hidden_states, dtype=np.float32).reshape(TOK, DM))
    wqT = np.ascontiguousarray(np.asarray(L0_attn_Wq, np.float32).T)
    wkT = np.ascontiguousarray(np.asarray(L0_attn_Wk, np.float32).T)
    wvT = np.ascontiguousarray(np.asarray(L0_attn_Wv, np.float32).T)
    woT = np.ascontiguousarray(np.asarray(L0_attn_Wo, np.float32).T)
    wbT = np.ascontiguousarray(np.asarray(L0_ffn_W_base, np.float32).T)
    wrT = np.ascontiguousarray(np.asarray(L0_ffn_W_router, np.float32).T)
    aT = np.ascontiguousarray(np.asarray(L0_ffn_A, np.float32).reshape(128, DM).T)
    bfT = np.ascontiguousarray(np.asarray(L0_ffn_B, np.float32).transpose(0, 2, 1).reshape(128, DM))
    bb = np.ascontiguousarray(np.asarray(L0_ffn_b_base, np.float32).reshape(1, DM))
    smalls = np.ascontiguousarray(np.concatenate([
        np.asarray(a, np.float32).reshape(-1) for a in
        (L0_attn_mu_q, L0_attn_sigma_q, L0_attn_eps_q,
         L0_attn_mu_kv, L0_attn_sigma_kv, L0_attn_eps_kv)]).reshape(1, 384))

    if _NC_CACHE[0] is None:
        _NC_CACHE[0] = _build_program()
    nc = _NC_CACHE[0]

    in_maps = [{
        "x": xf[c * TPC:(c + 1) * TPC],
        "wqT": wqT, "wkT": wkT, "wvT": wvT, "woT": woT, "wbT": wbT,
        "wrT": wrT, "aT": aT, "bfT": bfT, "bb": bb, "smalls": smalls,
    } for c in range(NCORES)]
    res = run_bass_kernel_spmd(nc, in_maps, core_ids=list(range(NCORES)))
    out = np.concatenate([r["out"] for r in res.results], axis=0)
    return out.reshape(4, 4096, 4096)


